# version 14
# speedup vs baseline: 1.0542x; 1.0361x over previous
import numpy as np
import ml_dtypes
from contextlib import ExitStack

import concourse.bass as bass
import concourse.mybir as mybir
from concourse.alu_op_type import AluOpType
from concourse.bass_utils import run_bass_kernel_spmd

F32 = mybir.dt.float32
BF16 = mybir.dt.bfloat16
AFT = mybir.ActivationFunctionType
Alu = AluOpType
BF = ml_dtypes.bfloat16
AX = mybir.AxisListType

T = 2048
D = 1024
NTT = 16
NFC = 8
NTC = 4
R = 512
FFN = 4096
INV = 1.0 / 10000.0 ** (np.arange(32, dtype=np.float64) / 32.0)

_DBG = ()


def _consts():
    c = {}
    c["ident_bf"] = np.eye(128, dtype=BF)
    c["ident_f32"] = np.eye(128, dtype=np.float32)
    t = np.arange(T, dtype=np.float64)
    ang = t[:, None] * INV[None, :]
    cos, sin = np.cos(ang), np.sin(ang)
    c["costf"] = np.tile(cos, (1, 32)).astype(BF)
    sinf = np.zeros((T, 1024), np.float64)
    for h in range(16):
        sinf[:, h * 64:h * 64 + 32] = -sin
        sinf[:, h * 64 + 32:h * 64 + 64] = sin
    c["sintf"] = sinf.astype(BF)
    dm = np.zeros((128, 2048), np.float32)
    kp = np.arange(128)[:, None]
    qf = np.arange(512)[None, :]
    for b in range(4):
        dm[:, b * 512:(b + 1) * 512] = (qf >= b * 128 + kp)
    c["diagmask"] = dm.astype(BF)
    u = np.arange(3968)[None, :]
    c["stairbig"] = ((u - 1920) < kp).astype(BF)
    c["riota"] = np.broadcast_to(np.arange(512, dtype=np.float32)[None, :], (128, 512)).copy()
    c["ones_f32row"] = np.ones((1, 128), np.float32)
    c["onesb"] = np.ones((128, 1), BF)
    c["onesbrow"] = np.ones((1, 128), BF)
    c["ones128"] = np.ones((128, 128), BF)
    c["costok"] = np.cos(ang).astype(BF)
    c["sintok"] = np.sin(ang).astype(BF)
    jj = np.arange(128)
    c["triu128"] = (jj[:, None] < jj[None, :]).astype(BF)
    c["thrc"] = (np.arange(16)[None, :] * 128 + jj[:, None]).astype(np.float32)
    hs64 = np.zeros((16, D), np.float32)
    for h in range(16):
        hs64[h, h * 64:(h + 1) * 64] = 1.0
    c["hs64"] = hs64.astype(BF)
    hs256 = np.zeros((4, D), np.float32)
    for h in range(4):
        hs256[h, h * 256:(h + 1) * 256] = 1.0
    c["hs256"] = hs256
    return c


class Ctr:
    def __init__(self, stack, nc, name):
        self.h = stack.enter_context(nc.semaphore(name))
        self.n = 0

    def inc(self, instr, k=1):
        instr.then_inc(self.h, k)
        self.n += k
        return instr


def ap_swap(tile, ncols, row0=0, nrows=128):
    return bass.AP(tile, row0 * 0 + 32,
                   [[ncols, nrows], [64, ncols // 64], [-32, 2], [1, 32]])


def ap_v65(tile, half):
    return bass.AP(tile, half * 8 * 65, [[1040, 128], [65, 8], [1, 64]])


_CHAIN_METHODS = {
    "tensor_tensor", "tensor_scalar", "tensor_copy", "tensor_reduce",
    "reciprocal", "activation", "copy", "select", "scalar_tensor_tensor",
    "memset", "iota", "tensor_add", "tensor_sub", "tensor_mul", "tensor_max",
    "reduce_sum", "reduce_max", "memzero", "add", "mul", "sign", "sqrt",
    "square",
}


class FakeIns:

    def __init__(self):
        self.incs = []

    def then_inc(self, sem, k):
        self.incs.append((sem, k))
        return self


class RecEng:

    def __init__(self, ops):
        self.ops = ops

    def __getattr__(self, name):
        def call(*a, **kw):
            fi = FakeIns()
            self.ops.append((name, a, kw, fi))
            return fi
        return call


class ChainEng:

    def __init__(self, eng, ctr):
        self._eng = eng
        self._ctr = ctr

    def __getattr__(self, name):
        fn = getattr(self._eng, name)
        if name not in _CHAIN_METHODS:
            return fn

        def wrapped(*a, **kw):
            if self._ctr.n > 0:
                self._eng.wait_ge(self._ctr.h, self._ctr.n)
            ins = fn(*a, **kw)
            self._ctr.inc(ins, 1)
            return ins
        return wrapped


class KB:

    def __init__(self, dbg=()):
        self.dbg = set(dbg)
        self.nc = bass.Bass("TRN2", target_bir_lowering=False)
        self.es = ExitStack()
        self.dbg_out = {}
        self.ops = {k: [] for k in "TSVGY"}

    def flush(self):
        CH = 1024
        with self.nc.Block() as blk:
            for key, mk in (("T", blk.tensor), ("S", blk.scalar),
                            ("V", blk.vector), ("G", blk.gpsimd),
                            ("Y", blk.sync)):
                ops = self.ops[key]
                for c0 in range(0, len(ops), CH):
                    def run(e, chunk=ops[c0:c0 + CH]):
                        for name, a, kw, fi in chunk:
                            ins = getattr(e, name)(*a, **kw)
                            for sem, k in fi.incs:
                                ins.then_inc(sem, k)
                    mk(run)

    def sb(self, name, shape, dt):
        return self.es.enter_context(self.nc.sbuf_tensor(name, shape, dt))

    def ps(self, name, shape, dt=F32):
        return self.es.enter_context(self.nc.psum_tensor(name, shape, dt))

    def mkdbg(self, name, shape, dtype=F32):
        if name in self.dbg:
            self.dbg_out[name] = self.nc.dram_tensor(
                "dbg_" + name, shape, dtype, kind="ExternalOutput").ap()
        return self.dbg_out.get(name)

    def dbg_dump(self, name, tiles, gate=None):
        ap = self.dbg_out.get(name)
        if ap is None:
            return
        sDB = self.sDB

        def _(g):
            if gate is not None:
                gate(g)
            for i, tl in enumerate(tiles):
                sDB.inc(g.dma_start(out=ap.rearrange("n p c -> n p c")[i],
                                    in_=tl if isinstance(tl, bass.AP) else tl[:, :]), 16)
            g.wait_ge(sDB.h, sDB.n)
        self.GPS(_)

    def barrier(self, engs="TSV"):
        sB = self.sB
        if self.dbg and "G" not in engs:
            engs = engs + "G"
        base = sB.n
        k = len(engs)
        own = {"T": self.sT, "S": self.sS, "V": self.sV, "G": self.sG}
        for e in engs:
            def _(eng, e=e):
                c = own[e]
                if c.n > 0:
                    eng.wait_ge(c.h, c.n)
                sB.inc(eng.nop())
            {"T": self.TEN, "S": self.SCA, "V": self.VEC, "G": self.GPS}[e](_)
        for e in engs:
            def _(eng):
                eng.wait_ge(sB.h, base + k)
            {"T": self.TEN, "S": self.SCA, "V": self.VEC, "G": self.GPS}[e](_)


def build_program(dbg=(), ncores=8):
    kb = KB(dbg)
    nc, es = kb.nc, kb.es

    x_h = nc.dram_tensor("x", [T, D], F32, kind="ExternalInput").ap()
    out_h = nc.dram_tensor("out", [T, D], F32, kind="ExternalOutput").ap()
    W = {}
    wspec = [
        ("qT_w", [D, D]), ("qnw", [1, D]),
        ("w0T", [4 * 256, 512]), ("w1T", [4 * 512, 256]), ("w2T", [4 * 256, 512]),
        ("wq1", [D, D]), ("wk1", [D, D]), ("wv1", [D, D]), ("wo1", [D, D]),
        ("ln11", [1, D]), ("ln21", [1, D]),
        ("wg1", [D, FFN]), ("wu1", [D, FFN]), ("wd1", [FFN, D]),
        ("wq2", [D, D]), ("wk2", [D, D]), ("wv2", [D, D]), ("wo2", [D, D]),
        ("ln12", [1, D]), ("ln22", [1, D]),
        ("wg2", [D, FFN]), ("wu2", [D, FFN]), ("wd2", [FFN, D]),
        ("pnw", [1, D]), ("prior_w", [D, 2 * D]),
    ]
    for nm, sh in wspec:
        W[nm] = nc.dram_tensor(nm, sh, F32, kind="ExternalInput").ap()
    cns = _consts()
    CN = {}
    for nm, arr in cns.items():
        dt = BF16 if arr.dtype == BF else F32
        CN[nm] = nc.dram_tensor("c_" + nm, list(arr.shape), dt, kind="ExternalInput").ap()
    cc_in = nc.dram_tensor("cc_in", [1, 128], F32)
    cc_out = nc.dram_tensor("cc_out", [1, 128], F32)

    K = {}
    for nm in ("ident_bf", "ident_f32", "diagmask", "ones_f32row", "onesb",
               "onesbrow", "hs64"):
        arr = cns[nm]
        K[nm] = kb.sb("k_" + nm, list(arr.shape), BF16 if arr.dtype == BF else F32)
    VCOL = {nm: kb.sb("vc_" + nm, [128, NFC], F32)
            for nm in ("qnw", "ln11", "ln21", "ln12", "ln22", "pnw")}
    xnTall = kb.sb("xnTall", [128, NFC * T], BF16)
    xn_s = lambda fc, c0, c1: xnTall[:, fc * T + c0:fc * T + c1]
    m16 = kb.sb("m16", [128, NTT], F32)
    r16 = kb.sb("r16", [128, NTT], F32)
    rs1c = kb.sb("rs1c", [128, NTT], F32)
    rs1b = kb.sb("rs1b", [1, T], BF16)
    CSTAT = kb.sb("CSTAT", [128, 80], F32)
    ccsb = kb.sb("ccsb", [1, 128], F32)
    GSEL = kb.sb("GSEL", [128, 4], F32)

    for s_ in ("sT", "sS", "sV", "sG", "sW", "sX", "sXo", "sC", "sB", "sWD"):
        setattr(kb, s_, Ctr(es, nc, s_))
    sT, sS, sV, sG, sW, sX, sC, sB = (kb.sT, kb.sS, kb.sV, kb.sG, kb.sW, kb.sX,
                                      kb.sC, kb.sB)
    sWD = kb.sWD
    kb.sK = Ctr(es, nc, "sK")
    kb.sDB = Ctr(es, nc, "sDB")
    sK = kb.sK
    sDX = [Ctr(es, nc, f"sDX{i}") for i in range(4)]
    sDR = [Ctr(es, nc, f"sDR{i}") for i in range(2)]
    sDF = [Ctr(es, nc, f"sDF{i}") for i in range(NFC)]
    kb.sK2 = Ctr(es, nc, "sK2")
    kb.sW2 = Ctr(es, nc, "sW2")
    kb.sPC = {nm: Ctr(es, nc, "sPC_" + nm)
              for nm in ("wv2", "wq2", "wk2", "wo2", "wg2", "wu2", "wd2")}
    kb.sWD2 = Ctr(es, nc, "sWD2")
    kb.sDF2 = [Ctr(es, nc, f"sDF2{i}") for i in range(NFC)]

    _tc = RecEng(kb.ops["T"])
    _sc = ChainEng(RecEng(kb.ops["S"]), sS)
    _vc = ChainEng(RecEng(kb.ops["V"]), sV)
    _gc = ChainEng(RecEng(kb.ops["G"]), sG)
    _yc = RecEng(kb.ops["Y"])
    kb.TEN = lambda f: f(_tc)
    kb.SCA = lambda f: f(_sc)
    kb.VEC = lambda f: f(_vc)
    kb.GPS = lambda f: f(_gc)
    kb.SYN = lambda f: f(_yc)
    TEN, SCA, VEC, GPS, SYN = kb.TEN, kb.SCA, kb.VEC, kb.GPS, kb.SYN

    def _(g):
        for nm in K:
            sK.inc(g.dma_start(out=K[nm][:, :], in_=CN[nm]), 16)
        for nm in VCOL:
            sK.inc(g.dma_start(out=VCOL[nm][:, :],
                               in_=W[nm].rearrange("o (f p) -> p (o f)", p=128),
                               allow_slow_non_contiguous=True), 16)
    GPS(_)
    W_CONST = sK.n

    p0 = ExitStack()
    x_sb = [p0.enter_context(nc.sbuf_tensor(f"x{i}", [128, D], F32)) for i in range(4)]
    xtok = [p0.enter_context(nc.sbuf_tensor(f"xt{i}", [128, D], BF16)) for i in range(4)]
    sqscr = p0.enter_context(nc.sbuf_tensor("sqscr", [128, D], F32))
    st0 = p0.enter_context(nc.sbuf_tensor("st0", [128, 3 * NTT], F32))
    psT0 = p0.enter_context(nc.psum_tensor("psT0", [128, 2048], BF16))
    x_t = x_h.rearrange("(n p) d -> n p d", p=128)

    psRow = p0.enter_context(nc.psum_tensor("psRow", [1, T], F32))

    TS_V = {}
    XT_S = {}
    XT_T = {}
    EV_V = {}

    xp1 = {}

    def _(sy):
        for i in range(4):
            sDX[i].inc(sy.dma_start(out=x_sb[i][:, :], in_=x_t[i]), 16)
            xp1[i] = sDX[i].n
    SYN(_)

    for i in range(NTT):
        def _(v, i=i):
            v.wait_ge(sDX[i % 4].h, xp1[i])
            v.tensor_reduce(out=st0[:, i:i + 1], in_=x_sb[i % 4][:, :],
                            axis=AX.X, op=Alu.add)
        VEC(_)
        V_SM = sV.n

        def _(s, i=i, V_SM=V_SM):
            s.wait_ge(sDX[i % 4].h, xp1[i])
            s.activation(out=sqscr[:, :], in_=x_sb[i % 4][:, :],
                         func=AFT.Square,
                         accum_out=st0[:, NTT + i:NTT + i + 1])
        SCA(_)
        S_SQ = sS.n

        def _(v, i=i, S_SQ=S_SQ):
            v.wait_ge(sS.h, S_SQ)
            tmp = st0[:, 2 * NTT + i:2 * NTT + i + 1]
            v.tensor_scalar(out=m16[:, i:i + 1], in0=st0[:, i:i + 1],
                            scalar1=1.0 / D, scalar2=None,
                            op0=Alu.mult, op1=Alu.bypass)
            v.tensor_tensor(out=tmp, in0=m16[:, i:i + 1], in1=m16[:, i:i + 1],
                            op=Alu.mult)
            v.tensor_scalar(out=tmp, in0=tmp, scalar1=-1.0, scalar2=1e-5,
                            op0=Alu.mult, op1=Alu.add)
            v.scalar_tensor_tensor(out=tmp, in0=st0[:, NTT + i:NTT + i + 1],
                                   scalar=1.0 / D, in1=tmp,
                                   op0=Alu.mult, op1=Alu.add)
            v.reciprocal(out=tmp, in_=tmp)
        VEC(_)
        V_VAR = sV.n

        def _(s, i=i, V_VAR=V_VAR):
            s.wait_ge(sV.h, V_VAR)
            s.activation(out=r16[:, i:i + 1], in_=st0[:, 2 * NTT + i:2 * NTT + i + 1],
                         func=AFT.Sqrt)
        SCA(_)
        S_R = sS.n

        def _(v, i=i, S_R=S_R):
            v.wait_ge(sS.h, S_R)
            if i >= 4:
                v.wait_ge(sT.h, XT_T[i - 4])
                v.wait_ge(sS.h, XT_S[i - 4])
            v.tensor_scalar(out=xtok[i % 4][:, :], in0=x_sb[i % 4][:, :],
                            scalar1=m16[:, i:i + 1], scalar2=r16[:, i:i + 1],
                            op0=Alu.subtract, op1=Alu.mult)
        VEC(_)
        TS_V[i] = sV.n

        if i + 4 < NTT:
            def _(sy, i=i, V_TS=TS_V[i]):
                sy.wait_ge(sV.h, V_TS)
                sDX[i % 4].inc(sy.dma_start(out=x_sb[i % 4][:, :], in_=x_t[i + 4]), 16)
            SYN(_)
            xp1[i + 4] = sDX[i % 4].n

        def _(s, i=i, V_TS=TS_V[i]):
            s.wait_ge(sV.h, V_TS)
            s.activation(out=sqscr[:, :], in_=xtok[i % 4][:, :],
                         func=AFT.Square,
                         accum_out=CSTAT[:, 64 + i:64 + i + 1])
        SCA(_)
        XT_S[i] = sS.n

        def _(t, i=i, V_TS=TS_V[i]):
            if i == 0:
                t.wait_ge(sK.h, W_CONST)
            t.wait_ge(sV.h, V_TS)
            if i >= 2:
                t.wait_ge(sV.h, EV_V[i - 2])
            half = (i % 2) * 1024
            for fc in range(NFC):
                ins = t.transpose(psT0[:, half + fc * 128:half + (fc + 1) * 128],
                                  xtok[i % 4][:, fc * 128:(fc + 1) * 128],
                                  K["ident_bf"][:, :])
            sT.inc(ins)
        TEN(_)
        XT_T[i] = sT.n

        def _(v, i=i, T_TR=XT_T[i]):
            v.wait_ge(sT.h, T_TR)
            half = (i % 2) * 1024
            v.tensor_copy(
                out=bass.AP(xnTall, i * 128, [[NFC * T, 128], [T, NFC], [1, 128]]),
                in_=psT0[:, half:half + 1024])
        VEC(_)
        EV_V[i] = sV.n
    V_XNT = sV.n
    S_RSA = sS.n

    kb.mkdbg("xnT", [NFC, 128, T])
    kb.dbg_dump("xnT", [xnTall[:, f * T:(f + 1) * T] for f in range(NFC)],
                gate=lambda sy: sy.wait_ge(sV.h, V_XNT))

    def _(v):
        v.wait_ge(sS.h, S_RSA)
        v.tensor_scalar(out=rs1c[:, :], in0=CSTAT[:, 64:64 + NTT],
                        scalar1=1.0 / D, scalar2=1e-6, op0=Alu.mult, op1=Alu.add)
        v.reciprocal(out=rs1c[:, :], in_=rs1c[:, :])
    VEC(_)
    V_RS1A = sV.n

    def _(s):
        s.wait_ge(sV.h, V_RS1A)
        s.activation(out=rs1c[:, :], in_=rs1c[:, :], func=AFT.Sqrt)
    SCA(_)
    S_RS1 = sS.n

    def _(t):
        t.wait_ge(sS.h, S_RS1)
        for i in range(NTT):
            ins = t.transpose(psRow[0:1, i * 128:(i + 1) * 128], rs1c[:, i:i + 1],
                              K["ident_f32"][:, :])
        sT.inc(ins)
    TEN(_)
    T_RS1C = sT.n

    def _(v):
        v.wait_ge(sT.h, T_RS1C)
        v.tensor_copy(out=rs1b[:, :], in_=psRow[:, :])
    VEC(_)
    V_RS1 = sV.n

    kb.mkdbg("rs1c", [1, 128, NTT])
    kb.dbg_dump("rs1c", [rs1c], gate=lambda sy: sy.wait_ge(sV.h, V_RS1))
    p0.close()

    p2a = ExitStack()
    qTall = p2a.enter_context(nc.sbuf_tensor("qTall", [128, NFC * T], BF16))
    kTall = p2a.enter_context(nc.sbuf_tensor("kTall", [128, NFC * T], BF16))
    vtok = [p2a.enter_context(nc.sbuf_tensor(f"vt{i}", [128, 1040], BF16))
            for i in range(NTT)]
    p2ab = ExitStack()
    WQ = [p2ab.enter_context(nc.sbuf_tensor(f"wq_{i}", [128, 512], BF16)) for i in range(NFC)]
    WK = [p2ab.enter_context(nc.sbuf_tensor(f"wk_{i}", [128, 512], BF16)) for i in range(NFC)]
    WV = [p2ab.enter_context(nc.sbuf_tensor(f"wv_{i}", [128, 512], BF16)) for i in range(NFC)]
    qbuf = [p2ab.enter_context(nc.sbuf_tensor(f"qb{i}", [128, 512], BF16)) for i in range(2)]
    kbuf = [p2ab.enter_context(nc.sbuf_tensor(f"kb{i}", [128, 512], BF16)) for i in range(2)]
    qrp = [p2ab.enter_context(nc.sbuf_tensor(f"qr{i}", [128, 512], BF16)) for i in range(2)]
    krp = [p2ab.enter_context(nc.sbuf_tensor(f"kr{i}", [128, 512], BF16)) for i in range(2)]
    ra = p2ab.enter_context(nc.sbuf_tensor("ra", [128, 512], BF16))
    rb = p2ab.enter_context(nc.sbuf_tensor("rb", [128, 512], BF16))
    cosbuf = [p2ab.enter_context(nc.sbuf_tensor(f"cb{i}", [128, 512], BF16)) for i in range(2)]
    sinbuf = [p2ab.enter_context(nc.sbuf_tensor(f"sb{i}", [128, 512], BF16)) for i in range(2)]
    psQ = [p2ab.enter_context(nc.psum_tensor(f"psQ{i}", [128, 512], F32))
           for i in range(2)]
    psK = [p2ab.enter_context(nc.psum_tensor(f"psK{i}", [128, 512], F32))
           for i in range(2)]
    psVp = [p2ab.enter_context(nc.psum_tensor(f"psVp{i}", [128, 512], F32))
            for i in range(2)]
    psT1 = p2ab.enter_context(nc.psum_tensor("psT1", [128, 512], BF16))
    psT2 = p2ab.enter_context(nc.psum_tensor("psT2", [128, 512], BF16))

    costf_t = CN["costf"].rearrange("(n p) d -> n p d", p=128)
    sintf_t = CN["sintf"].rearrange("(n p) d -> n p d", p=128)

    def _(g):
        for i in range(NTT):
            ins = g.memset(bass.AP(vtok[i], 64, [[1040, 128], [65, 16]]), 1.0)
    GPS(_)
    G_ONES = sG.n

    for half in range(2):
        W_H0 = sW.n

        def _(g, half=half):
            if half == 1:
                g.wait_ge(sT.h, LAST_MM)
            for dinc in range(NFC):
                c0 = half * 512
                sW.inc(g.dma_start(out=WQ[dinc][:, :],
                                   in_=W["wq1"][dinc * 128:(dinc + 1) * 128, c0:c0 + 512]), 16)
                sW.inc(g.dma_start(out=WK[dinc][:, :],
                                   in_=W["wk1"][dinc * 128:(dinc + 1) * 128, c0:c0 + 512]), 16)
                sW.inc(g.dma_start(out=WV[dinc][:, :],
                                   in_=W["wv1"][dinc * 128:(dinc + 1) * 128, c0:c0 + 512]), 16)
        GPS(_)
        W_H = sW.n

        vc_ln11 = VCOL["ln11"]
        W_LN = sW.n

        def _(v, half=half, W_LN=W_LN):
            v.wait_ge(sW.h, W_LN)
            for dinc in range(NFC):
                sc = vc_ln11[:, dinc:dinc + 1]
                v.tensor_scalar(out=WQ[dinc][:, :], in0=WQ[dinc][:, :], scalar1=sc,
                                scalar2=None, op0=Alu.mult, op1=Alu.bypass)
                v.tensor_scalar(out=WK[dinc][:, :], in0=WK[dinc][:, :], scalar1=sc,
                                scalar2=None, op0=Alu.mult, op1=Alu.bypass)
                ins = v.tensor_scalar(out=WV[dinc][:, :], in0=WV[dinc][:, :], scalar1=sc,
                                      scalar2=None, op0=Alu.mult, op1=Alu.bypass)
        VEC(_)
        V_WSC = sV.n

        V_VB = sV.n

        def _(sy, half=half, V_VB=V_VB):
            for tt in range(NTT):
                if tt >= 2:
                    sy.wait_ge(sV.h, V_VB + 8 * (tt - 2) + 6)
                else:
                    sy.wait_ge(sV.h, V_VB)
                sDR[tt % 2].inc(sy.dma_start(
                    out=cosbuf[tt % 2][:, :],
                    in_=costf_t[tt][:, half * 512:half * 512 + 512]), 16)
                sDR[tt % 2].inc(sy.dma_start(
                    out=sinbuf[tt % 2][:, :],
                    in_=sintf_t[tt][:, half * 512:half * 512 + 512]), 16)
        SYN(_)
        rp_w = [sDR[tt % 2].n - 32 * ((NTT - 1 - tt) // 2) for tt in range(NTT)]

        ev_done = [None] * NTT
        rp_q = [None] * NTT
        rp_k = [None] * NTT
        trq = [None] * NTT
        trk = [None] * NTT
        tre_q = [None] * NTT
        tre_k = [None] * NTT
        mm_done = [None] * NTT

        for tt in range(NTT):
            def _(t, tt=tt, half=half):
                if tt == 0:
                    t.wait_ge(sV.h, V_WSC)
                    t.wait_ge(sG.h, G_ONES)
                if tt >= 2:
                    t.wait_ge(sS.h, ev_done[tt - 2])
                for dinc in range(NFC):
                    lhs = xnTall[:, dinc * T + tt * 128:dinc * T + (tt + 1) * 128]
                    t.matmul(psQ[tt % 2][:, :], lhs, WQ[dinc][:, :],
                             start=(dinc == 0), stop=(dinc == NFC - 1))
                    t.matmul(psK[tt % 2][:, :], lhs, WK[dinc][:, :],
                             start=(dinc == 0), stop=(dinc == NFC - 1))
                    ins = t.matmul(psVp[tt % 2][:, :], lhs, WV[dinc][:, :],
                                   start=(dinc == 0), stop=(dinc == NFC - 1))
                sT.inc(ins)
                mm_done[tt] = sT.n
            TEN(_)

            def _(s, tt=tt, half=half):
                s.wait_ge(sT.h, mm_done[tt])
                if tt >= 2:
                    s.wait_ge(sV.h, rp_k[tt - 2])
                rs = rs1c[:, tt:tt + 1]
                s.activation(out=qbuf[tt % 2][:, :], in_=psQ[tt % 2][:, :],
                             func=AFT.Identity, scale=rs)
                s.activation(out=kbuf[tt % 2][:, :], in_=psK[tt % 2][:, :],
                             func=AFT.Identity, scale=rs)
                s.activation(out=ap_v65(vtok[tt], half), in_=psVp[tt % 2][:, :],
                                    func=AFT.Identity, scale=rs)
                ev_done[tt] = sS.n
            SCA(_)

            def _(v, tt=tt):
                v.wait_ge(sS.h, ev_done[tt])
                v.wait_ge(sDR[tt % 2].h, rp_w[tt])
                qb, kb2 = qbuf[tt % 2], kbuf[tt % 2]
                v.tensor_tensor(out=ra[:, :], in0=qb[:, :], in1=cosbuf[tt % 2][:, :],
                                op=Alu.mult)
                v.tensor_tensor(out=rb[:, :], in0=ap_swap(qb, 512),
                                in1=sinbuf[tt % 2][:, :], op=Alu.mult)
                v.tensor_tensor(out=qrp[tt % 2][:, :], in0=ra[:, :],
                                       in1=rb[:, :], op=Alu.add)
                rp_q[tt] = sV.n
                v.tensor_tensor(out=ra[:, :], in0=kb2[:, :], in1=cosbuf[tt % 2][:, :],
                                op=Alu.mult)
                v.tensor_tensor(out=rb[:, :], in0=ap_swap(kb2, 512),
                                in1=sinbuf[tt % 2][:, :], op=Alu.mult)
                v.tensor_tensor(out=krp[tt % 2][:, :], in0=ra[:, :],
                                       in1=rb[:, :], op=Alu.add)
                rp_k[tt] = sV.n
            VEC(_)

            def _(t, tt=tt):
                t.wait_ge(sV.h, rp_q[tt])
                if tt >= 1:
                    t.wait_ge(sV.h, tre_q[tt - 1])
                for j in range(4):
                    ins = t.transpose(psT1[:, j * 128:(j + 1) * 128],
                                      qrp[tt % 2][:, j * 128:(j + 1) * 128],
                                      K["ident_bf"][:, :])
                sT.inc(ins)
                trq[tt] = sT.n
                t.wait_ge(sV.h, rp_k[tt])
                if tt >= 1:
                    t.wait_ge(sV.h, tre_k[tt - 1])
                for j in range(4):
                    ins = t.transpose(psT2[:, j * 128:(j + 1) * 128],
                                      krp[tt % 2][:, j * 128:(j + 1) * 128],
                                      K["ident_bf"][:, :])
                sT.inc(ins)
                trk[tt] = sT.n
            TEN(_)

            def _(v, tt=tt, half=half):
                off = tt * 128 + half * 4 * T
                v.wait_ge(sT.h, trq[tt])
                v.tensor_copy(
                    out=bass.AP(qTall, off, [[NFC * T, 128], [T, 4], [1, 128]]),
                    in_=psT1[:, :])
                tre_q[tt] = sV.n
                v.wait_ge(sT.h, trk[tt])
                v.tensor_copy(
                    out=bass.AP(kTall, off, [[NFC * T, 128], [T, 4], [1, 128]]),
                    in_=psT2[:, :])
                tre_k[tt] = sV.n
            VEC(_)
        LAST_MM = mm_done[NTT - 1]

    kb.barrier("TSV")
    P2A_B = sB.n
    kb.mkdbg("qT", [NFC, 128, T], BF16)
    kb.dbg_dump("qT", [qTall[:, f * T:(f + 1) * T] for f in range(NFC)],
                gate=lambda sy: sy.wait_ge(sB.h, P2A_B))
    kb.mkdbg("kT", [NFC, 128, T], BF16)
    kb.dbg_dump("kT", [kTall[:, f * T:(f + 1) * T] for f in range(NFC)],
                gate=lambda sy: sy.wait_ge(sB.h, P2A_B))
    kb.mkdbg("vtok", [NTT, 128, 1040], BF16)
    kb.dbg_dump("vtok", vtok, gate=lambda sy: sy.wait_ge(sB.h, P2A_B))
    p2ab.close()

    p2b0 = ExitStack()
    WO = [p2b0.enter_context(nc.sbuf_tensor(f"wo_{i}", [128, 1024], BF16, side="right"))
          for i in range(NFC)]
    oSc = [p2b0.enter_context(nc.sbuf_tensor(f"oSc{f}", [128, T], BF16, side="right"))
           for f in range(NFC)]
    p2b = ExitStack()
    expb = p2b.enter_context(nc.sbuf_tensor("expb", [128, 2048], BF16))
    SR = p2b.enter_context(nc.sbuf_tensor("SR", [16, T], F32))
    SRb = p2b.enter_context(nc.sbuf_tensor("SRb", [16, T], BF16))
    STG = p2b.enter_context(nc.sbuf_tensor("STG", [1, 512], F32))
    psA = p2b.enter_context(nc.psum_tensor("psA", [128, 2048], F32))
    psO = p2b.enter_context(nc.psum_tensor("psO", [65, 1024], F32))

    W_WO0 = sW.n

    def _(g):
        g.wait_ge(sB.h, sB.n)
        for i in range(NFC):
            sW.inc(g.dma_start(out=WO[i][:, :],
                               in_=W["wo1"][i * 128:(i + 1) * 128, :]), 16)
    GPS(_)
    W_WO = sW.n

    seq = [(h, qc) for h in range(16) for qc in range(NTC)]
    evo_done = {}
    evs_done = {}
    stg_dma = {}
    exp_by_par = {0: 0, 1: 0}
    av_by_par = {0: 0, 1: 0}

    for si, (h, qc) in enumerate(seq):
        ng = 2 * qc + 2
        ex_done = [None] * ng
        msk_done = [None] * ng
        av_done = [None] * ng
        r0 = (h % 2) * 64
        cb = (h // 2) * T

        for g_ in range(ng + 1):
            if g_ < ng:
                def _(t, h=h, qc=qc, g=g_, si=si, r0=r0, cb=cb):
                    t.wait_ge(sS.h, exp_by_par[g % 2])
                    if g == 0 and si == 0:
                        t.wait_ge(sB.h, sB.n)
                    for j, kt in enumerate((2 * g, 2 * g + 1)):
                        ins = t.matmul(
                            psA[:, ((g % 2) * 2 + j) * 512:((g % 2) * 2 + j + 1) * 512],
                            kTall[r0:r0 + 64, cb + kt * 128:cb + (kt + 1) * 128],
                            qTall[r0:r0 + 64, cb + qc * 512:cb + (qc + 1) * 512],
                            start=True, stop=True)
                    sT.inc(ins)
                TEN(_)
                sc_g = sT.n

                exj = [None, None]
                for j in range(2):
                    def _(s, g=g_, sc=sc_g, j=j, avp=av_by_par[g_ % 2]):
                        if j == 0:
                            s.wait_ge(sT.h, sc)
                            s.wait_ge(sT.h, avp)
                        s.activation(
                            out=expb[:, ((g % 2) * 2 + j) * 512:
                                     ((g % 2) * 2 + j + 1) * 512],
                            in_=psA[:, ((g % 2) * 2 + j) * 512:
                                    ((g % 2) * 2 + j + 1) * 512],
                            func=AFT.Exp, scale=0.125)
                    SCA(_)
                    exj[j] = sS.n
                ex_done[g_] = [exj[0], exj[1]]
                exp_by_par[g_ % 2] = exj[1]

                if g_ >= ng - 2:
                    mkj = [None, None]
                    for j in range(2):
                        def _(v, qc=qc, g=g_, j=j, ex=exj[j]):
                            v.wait_ge(sS.h, ex)
                            l = 2 * g - 4 * qc
                            v.tensor_tensor(
                                out=expb[:, ((g % 2) * 2 + j) * 512:
                                         ((g % 2) * 2 + j + 1) * 512],
                                in0=expb[:, ((g % 2) * 2 + j) * 512:
                                         ((g % 2) * 2 + j + 1) * 512],
                                in1=K["diagmask"][:, (l + j) * 512:(l + j + 1) * 512],
                                op=Alu.mult)
                        VEC(_)
                        mkj[j] = sV.n
                    msk_done[g_] = mkj

            if g_ >= 1:
                gg = g_ - 1

                def _(t, h=h, qc=qc, gg=gg, si=si, ng=ng, exs=ex_done[gg],
                      mks=msk_done[gg]):
                    if gg == 0 and si >= 2:
                        t.wait_ge(sV.h, evo_done[si - 2])
                        t.wait_ge(sV.h, evs_done[si - 2])
                    for j, kt in enumerate((2 * gg, 2 * gg + 1)):
                        if mks is not None:
                            t.wait_ge(sV.h, mks[j])
                        else:
                            t.wait_ge(sS.h, exs[j])
                        ins = t.matmul(
                            psO[:, (qc % 2) * 512:(qc % 2) * 512 + 512],
                            vtok[kt][:, h * 65:h * 65 + 65],
                            expb[:, ((gg % 2) * 2 + j) * 512:((gg % 2) * 2 + j + 1) * 512],
                            start=(gg == 0 and j == 0), stop=(gg == ng - 1 and j == 1))
                    sT.inc(ins)
                TEN(_)
                av_done[gg] = sT.n
                av_by_par[gg % 2] = sT.n

        AV = av_done[ng - 1]

        def _(v, h=h, qc=qc, AV=AV):
            v.wait_ge(sT.h, AV)
            v.tensor_copy(
                out=oSc[h // 2][(h % 2) * 64:(h % 2) * 64 + 64, qc * 512:(qc + 1) * 512],
                in_=psO[0:64, (qc % 2) * 512:(qc % 2) * 512 + 512])
        VEC(_)
        evo_done[si] = sV.n

        def _(v, h=h, qc=qc, AV=AV, si=si):
            v.wait_ge(sT.h, AV)
            if si >= 1:
                v.wait_ge(sX.h, stg_dma[si - 1])
            v.tensor_copy(out=STG[0:1, :],
                          in_=psO[64:65, (qc % 2) * 512:(qc % 2) * 512 + 512])
        VEC(_)
        evs_done[si] = sV.n

        def _(sy, h=h, qc=qc, ev=evs_done[si]):
            sy.wait_ge(sV.h, ev)
            sX.inc(sy.dma_start(out=SR[h:h + 1, qc * 512:(qc + 1) * 512],
                                in_=STG[0:1, :]), 16)
        SYN(_)
        stg_dma[si] = sX.n

    ATT_V, ATT_S = sV.n, sS.n

    SR_DMA = sX.n

    def _(v):
        v.wait_ge(sS.h, ATT_S)
        v.wait_ge(sX.h, SR_DMA)
        v.reciprocal(out=SR[:, :], in_=SR[:, :])
        v.tensor_copy(out=SRb[:, :], in_=SR[:, :])
    VEC(_)
    V_RECIP = sV.n

    T_BC0 = sT.n

    def _(t):
        t.wait_ge(sV.h, V_RECIP)
        for pt in range(NFC):
            if pt >= 1:
                t.wait_ge(sV.h, V_RECIP + pt)
            for ch in range(NTC):
                ins = t.matmul(psA[:, ch * 512:(ch + 1) * 512],
                               K["hs64"][:, pt * 128:(pt + 1) * 128],
                               SRb[:, ch * 512:(ch + 1) * 512],
                               start=True, stop=True)
            sT.inc(ins)
    TEN(_)

    def _(v):
        for pt in range(NFC):
            v.wait_ge(sT.h, T_BC0 + pt + 1)
            v.tensor_tensor(out=oSc[pt][:, :], in0=oSc[pt][:, :],
                                   in1=psA[:, :], op=Alu.mult)
    VEC(_)
    V_OSC = sV.n

    kb.mkdbg("oSc", [NFC, 128, T], BF16)
    kb.dbg_dump("oSc", oSc, gate=lambda sy: sy.wait_ge(sV.h, V_OSC))
    kb.mkdbg("serow", [1, 16, T])
    kb.dbg_dump("serow", [SR], gate=lambda sy: sy.wait_ge(sV.h, V_RECIP))

    kb.barrier("TSV")
    p2b.close()
    p2a.close()

    B_FFN1G = sB.n
    SUM = kb.sb("SUM", [128, NFC * T], F32)
    sum_s = lambda fc, c0, c1: SUM[:, fc * T + c0:fc * T + c1]
    p2c = ExitStack()
    psW = p2c.enter_context(nc.psum_tensor("psW", [128, 2048], F32))
    evw = 0

    EVW = {}
    for u, (fo, tch) in enumerate([(fo, tch) for fo in range(NFC) for tch in range(2)]):
        def _(t, u=u, fo=fo, tch=tch):
            if u == 0:
                t.wait_ge(sW.h, W_WO)
                t.wait_ge(sB.h, sB.n)
            if u >= 2:
                t.wait_ge(sS.h, EVW[u - 2])
            for tc in (2 * tch, 2 * tch + 1):
                for dinc in range(NFC):
                    ins = t.matmul(psW[:, (u % 2) * 1024 + (tc % 2) * 512:
                                       (u % 2) * 1024 + (tc % 2) * 512 + 512],
                                   WO[dinc][:, fo * 128:(fo + 1) * 128],
                                   oSc[dinc][:, tc * 512:(tc + 1) * 512],
                                   start=(dinc == 0), stop=(dinc == NFC - 1))
            sT.inc(ins)
        TEN(_)
        mmw = sT.n

        def _(s, u=u, fo=fo, tch=tch, mmw=mmw):
            s.wait_ge(sT.h, mmw)
            for tc in (2 * tch, 2 * tch + 1):
                ins = s.copy(out=sum_s(fo, tc * 512, (tc + 1) * 512),
                             in_=psW[:, (u % 2) * 1024 + (tc % 2) * 512:
                                     (u % 2) * 1024 + (tc % 2) * 512 + 512])
        SCA(_)
        EVW[u] = sS.n

    kb.barrier("TSV")
    p2b0.close()
    p2c.close()
    S_WO_ALL = sS.n
    kb.mkdbg("sum_o", [NFC, 128, T])
    _b1 = sB.n
    kb.dbg_dump("sum_o", [SUM[:, f * T:(f + 1) * T] for f in range(NFC)],
                gate=lambda sy: sy.wait_ge(sB.h, _b1))

    p2d = ExitStack()
    WG = [p2d.enter_context(nc.sbuf_tensor(f"wg_{i}", [128, 1024], BF16)) for i in range(NFC)]
    WU = [p2d.enter_context(nc.sbuf_tensor(f"wu_{i}", [128, 1024], BF16)) for i in range(NFC)]
    WD = [p2d.enter_context(nc.sbuf_tensor(f"wd_{i}", [128, 1024], BF16)) for i in range(NFC)]
    h2T = [p2d.enter_context(nc.sbuf_tensor(f"h2T{f}", [128, 1024], BF16)) for f in range(NFC)]
    slab = [p2d.enter_context(nc.sbuf_tensor(f"slab{f}", [128, 1024], BF16)) for f in range(NFC)]
    x1scr = [p2d.enter_context(nc.sbuf_tensor(f"x1s{f}", [128, 512], BF16)) for f in range(NFC)]
    sqd = [p2d.enter_context(nc.sbuf_tensor(f"sqd{i}", [128, 512], BF16)) for i in range(2)]
    gscr = p2d.enter_context(nc.sbuf_tensor("gscr", [128, 1024], BF16))
    h2row = p2d.enter_context(nc.sbuf_tensor("h2row", [1, 512], F32))
    psG = p2d.enter_context(nc.psum_tensor("psG", [128, 1024], F32))
    psU = p2d.enter_context(nc.psum_tensor("psU", [128, 1024], F32))
    FB1 = dict(WG=WG, WU=WU, WD=WD, x1=x1scr, sqd=sqd, h2T=h2T, slab=slab,
               gscr=gscr, h2row=h2row, psG=psG, psU=psU)

    def emit_ffn(nt_tok, wg_r, wu_r, wd_r, ln2key, sum_fn, xn_fn, tag, FB,
                 wd_late=False, hw=False, gate0=None):
        DF = kb.sDF2 if hw else sDF
        WDC = kb.sWD2 if hw else kb.sWD
        LENG = SYN if hw else GPS
        """FFN over nt_tok tokens (multiple of 1024 for dec1; 512 for dec2).

        sum_fn(fc, cs) / xn_fn(fc, cs): [128, 512] column APs at token offset cs.
        Token chunk = 1024 (dec1) or 512 (dec2)."""
        tcw = 1024 if nt_tok % 1024 == 0 else 512
        n_tc = nt_tok // tcw
        prev_gu = {"n": None}
        prev_wd = {"n": None}
        nth = tcw // 512
        fq_range = FFN // (8 * 128)
        prev_gu = {"n": None}
        prev_wd = {"n": None}
        V_FFN_B = sB.n

        for tc in range(n_tc):
            ptc = ExitStack()
            psSt = ptc.enter_context(nc.psum_tensor(f"psSt{tag}{tc}", [1, 512], F32))
            psB2 = ptc.enter_context(nc.psum_tensor(f"psB{tag}{tc}", [128, 512], F32))
            for sub in range(nth):
                cs = tc * tcw + sub * 512

                def _(v, cs=cs, tc=tc, sub=sub):
                    if tc == 0 and sub == 0:
                        v.wait_ge(sB.h, V_FFN_B)
                    if prev_wd["n"] is not None:
                        v.wait_ge(sT.h, prev_wd["n"])
                    for fc in range(NFC):
                        ins = v.tensor_tensor(out=FB['x1'][fc][:, :],
                                              in0=sum_fn(fc, cs),
                                              in1=xn_fn(fc, cs), op=Alu.add)
                VEC(_)
                V_X1 = sV.n

                S_SQD = sS.n
                T_SQD = sT.n

                def _(v, V_X1=V_X1, T_SQD=T_SQD):
                    for fc in range(NFC):
                        if fc >= 2:
                            v.wait_ge(sT.h, T_SQD + fc - 1)
                        v.tensor_tensor(out=FB['sqd'][fc % 2][:, :],
                                        in0=FB['x1'][fc][:, :],
                                        in1=FB['x1'][fc][:, :], op=Alu.mult)
                VEC(_)
                S_SQD = sV.n - NFC

                def _(t, S_SQD=S_SQD):
                    for fc in range(NFC):
                        t.wait_ge(sV.h, S_SQD + fc + 1)
                        sT.inc(t.matmul(psSt[:, :], K["onesb"][:, 0:1],
                                        FB['sqd'][fc % 2][:, :],
                                        start=(fc == 0), stop=(fc == NFC - 1)))
                TEN(_)
                T_ST = sT.n

                def _(v, T_ST=T_ST):
                    v.wait_ge(sT.h, T_ST)
                    v.tensor_scalar(out=FB['h2row'][:, :], in0=psSt[:, :], scalar1=1.0 / D,
                                    scalar2=1e-6, op0=Alu.mult, op1=Alu.add)
                    v.reciprocal(out=FB['h2row'][:, :], in_=FB['h2row'][:, :])
                VEC(_)
                V_H2A = sV.n

                def _(s, V_H2A=V_H2A):
                    s.wait_ge(sV.h, V_H2A)
                    s.activation(out=FB['h2row'][:, :], in_=FB['h2row'][:, :], func=AFT.Sqrt)
                SCA(_)
                S_H2R = sS.n

                def _(t, S_H2R=S_H2R):
                    t.wait_ge(sS.h, S_H2R)
                    sT.inc(t.matmul(psB2[:, :], K["ones_f32row"][:, :], FB['h2row'][:, :],
                                    start=True, stop=True))
                TEN(_)
                T_H2B = sT.n

                def _(v, sub=sub, T_H2B=T_H2B):
                    v.wait_ge(sT.h, T_H2B)
                    for fc in range(NFC):
                        ins = v.scalar_tensor_tensor(
                            out=FB['h2T'][fc][:, sub * 512:(sub + 1) * 512],
                            in0=FB['x1'][fc][:, :], scalar=VCOL[ln2key][:, fc:fc + 1],
                            in1=psB2[:, :], op0=Alu.mult, op1=Alu.mult)
                VEC(_)
            V_H2 = sV.n
            ptc.close()
            pq = ExitStack()
            psD = pq.enter_context(nc.psum_tensor(f"psD{tag}{tc}", [128, 2 * tcw], F32))

            for fq in range(fq_range):
                _sb_gate = sB.n if gate0 is None else gate0

                def _(g, fq=fq, _sb_gate=_sb_gate, _pgu=prev_gu["n"],
                      _pwd=prev_wd["n"]):
                    if _pgu is not None:
                        g.wait_ge(sT.h, _pgu)
                    else:
                        g.wait_ge(sB.h, _sb_gate)
                        if hw:
                            for _w in ("wg2", "wu2", "wd2"):
                                g.wait_ge(kb.sPC[_w].h, PC[_w])
                    if wd_late and _pwd is not None:
                        g.wait_ge(sT.h, _pwd)
                    for fc in range(NFC):
                        col0 = (fq * 8 + fc) * 128
                        DF[fc].inc(g.dma_start(out=FB['WG'][fc][:, :],
                                               in_=wg_r[:, :, col0:col0 + 128]), 16)
                        DF[fc].inc(g.dma_start(out=FB['WU'][fc][:, :],
                                               in_=wu_r[:, :, col0:col0 + 128]), 16)
                    if not wd_late:
                        if _pwd is not None:
                            g.wait_ge(sT.h, _pwd)
                        for fc in range(NFC):
                            WDC.inc(g.dma_start(out=FB['WD'][fc][:, :],
                                                in_=wd_r[fq * 8 + fc]), 16)
                LENG(_)
                gu_w = [DF[fc].n for fc in range(NFC)]
                W_WD = WDC.n

                silu_d = [None] * NFC
                tsl_d = [None] * NFC
                gu_mm = [None] * NFC
                for fc in range(NFC):
                    def _(t, fc=fc, V_H2=V_H2, nth=nth, DF=DF):
                        t.wait_ge(DF[fc].h, gu_w[fc])
                        if fc >= 1:
                            t.wait_ge(sS.h, silu_d[fc - 1])
                            t.wait_ge(sV.h, tsl_d[fc - 1])
                        if fc == 0:
                            t.wait_ge(sV.h, V_H2)
                        for dinc in range(NFC):
                            for th in range(nth):
                                ins = t.matmul(FB['psG'][:, th * 512:(th + 1) * 512],
                                               FB['WG'][fc][:, dinc * 128:(dinc + 1) * 128],
                                               FB['h2T'][dinc][:, th * 512:(th + 1) * 512],
                                               start=(dinc == 0), stop=(dinc == NFC - 1))
                                ins = t.matmul(FB['psU'][:, th * 512:(th + 1) * 512],
                                               FB['WU'][fc][:, dinc * 128:(dinc + 1) * 128],
                                               FB['h2T'][dinc][:, th * 512:(th + 1) * 512],
                                               start=(dinc == 0), stop=(dinc == NFC - 1))
                        sT.inc(ins)
                    TEN(_)
                    gu_mm[fc] = sT.n

                    def _(s, fc=fc, nth=nth):
                        s.wait_ge(sT.h, gu_mm[fc])
                        s.activation(out=FB['gscr'][:, 0:512 * nth],
                                            in_=FB['psG'][:, 0:512 * nth], func=AFT.Silu)
                    SCA(_)
                    silu_d[fc] = sS.n

                    def _(v, fc=fc, nth=nth):
                        v.wait_ge(sS.h, silu_d[fc])
                        v.tensor_tensor(out=FB['slab'][fc][:, 0:512 * nth],
                                               in0=FB['psU'][:, 0:512 * nth],
                                               in1=FB['gscr'][:, 0:512 * nth], op=Alu.mult)
                    VEC(_)
                    tsl_d[fc] = sV.n
                prev_gu["n"] = sT.n
                if wd_late:
                    def _(g, fq=fq, _pgu=prev_gu["n"]):
                        g.wait_ge(sT.h, _pgu)
                        for fc in range(NFC):
                            WDC.inc(g.dma_start(out=FB['WD'][fc][:, :],
                                                in_=wd_r[fq * 8 + fc]), 16)
                    LENG(_)
                    W_WD = WDC.n

                wd_ev = [None] * NFC
                for do in range(NFC):
                    def _(t, do=do, W_WD=W_WD, nth=nth, WDC=WDC):
                        if do == 0:
                            t.wait_ge(WDC.h, W_WD)
                            t.wait_ge(sV.h, tsl_d[NFC - 1])
                        if do >= 2:
                            t.wait_ge(sV.h, wd_ev[do - 2])
                        for fc in range(NFC):
                            for th in range(nth):
                                ins = t.matmul(
                                    psD[:, (do % 2) * tcw + th * 512:(do % 2) * tcw + (th + 1) * 512],
                                    FB['WD'][fc][:, do * 128:(do + 1) * 128],
                                    FB['slab'][fc][:, th * 512:(th + 1) * 512],
                                    start=(fc == 0), stop=(fc == NFC - 1))
                        sT.inc(ins)
                    TEN(_)
                    wd_mm = sT.n

                    def _(v, do=do, tc=tc, wd_mm=wd_mm, tcw=tcw, nth=nth):
                        v.wait_ge(sT.h, wd_mm)
                        for th in range(nth):
                            cs = tc * tcw + th * 512
                            ins = v.tensor_tensor(
                                out=sum_fn(do, cs),
                                in0=psD[:, (do % 2) * tcw + th * 512:(do % 2) * tcw + (th + 1) * 512],
                                in1=sum_fn(do, cs), op=Alu.add)
                    VEC(_)
                    wd_ev[do] = sV.n
                prev_wd["n"] = sT.n
            pq.close()

    wg_r = W["wg1"].rearrange("(dc p) f -> p dc f", p=128)
    wu_r = W["wu1"].rearrange("(dc p) f -> p dc f", p=128)
    wd_r = W["wd1"].rearrange("(n p) d -> n p d", p=128)
    emit_ffn(T, wg_r, wu_r, wd_r, "ln21",
             lambda fc, cs: sum_s(fc, cs, cs + 512),
             lambda fc, cs: xn_s(fc, cs, cs + 512), tag=0, FB=FB1,
             gate0=B_FFN1G)

    kb.barrier("TSV")
    p2d.close()
    kb.mkdbg("res1", [NFC, 128, T])
    _b2 = sB.n
    kb.dbg_dump("res1", [SUM[:, f * T:(f + 1) * T] for f in range(NFC)],
                gate=lambda sy: sy.wait_ge(sB.h, _b2))

    pselt = ExitStack()
    selTb = pselt.enter_context(nc.sbuf_tensor("selTb", [128, NFC * 512], BF16,
                                               side="right"))
    sel_s = lambda fc, c0, c1: selTb[:, fc * 512 + c0:fc * 512 + c1]
    p5a = ExitStack()
    QW = [p5a.enter_context(nc.sbuf_tensor(f"qw_{i}", [128, 1024], BF16, side="right"))
          for i in range(NFC)]
    qnwf = p5a.enter_context(nc.sbuf_tensor("qnwf", [128, 1024], BF16, side="right"))
    st5 = p5a.enter_context(nc.sbuf_tensor("st5", [128, 12 * NTT], F32, side="right"))
    psQ5 = p5a.enter_context(nc.psum_tensor("psQ5", [128, 1024], F32))
    p3 = ExitStack()
    PR = [p3.enter_context(nc.sbuf_tensor(f"pr_{i}", [128, 2048], BF16)) for i in range(NFC)]
    mu_sb = [p3.enter_context(nc.sbuf_tensor(f"mu{f}", [128, 512], BF16)) for f in range(NFC)]
    ph = [p3.enter_context(nc.sbuf_tensor(f"ph{f}", [128, 512], BF16)) for f in range(NFC)]
    ascr = p3.enter_context(nc.sbuf_tensor("ascr", [128, 512], F32))
    escr = p3.enter_context(nc.sbuf_tensor("escr", [128, 512], BF16))
    t1scr = p3.enter_context(nc.sbuf_tensor("t1scr", [128, 512], F32))
    ubf = [p3.enter_context(nc.sbuf_tensor(f"ubf{i}", [128, 512], BF16)) for i in range(2)]
    ubf2 = [p3.enter_context(nc.sbuf_tensor(f"ubf2_{i}", [128, 512], BF16)) for i in range(2)]
    pcrow = ExitStack()
    crow = pcrow.enter_context(nc.sbuf_tensor("crow", [1, T], F32, side="right"))
    ccol = pcrow.enter_context(nc.sbuf_tensor("ccol", [128, NTT], F32, side="right"))
    psP = p3.enter_context(nc.psum_tensor("psP", [128, 1024], F32))
    psCw = p3.enter_context(nc.psum_tensor("psCw", [1, 512], F32))
    psBB = p3.enter_context(nc.psum_tensor("psBB", [128, 512], F32))
    psCC = p3.enter_context(nc.psum_tensor("psCC", [128, NTT], F32))

    W_PR0 = sW.n
    FFN1_T = sT.n

    def _(g):
        g.wait_ge(sB.h, sB.n)
        src_p = W["prior_w"].rearrange("(n p) d -> n p d", p=128)
        for i in range(NFC):
            sW.inc(g.dma_start(out=PR[i][:, :], in_=src_p[i]), 16)
    GPS(_)
    W_PR = sW.n

    B_P3 = sB.n
    for tc in range(NTC):
        cs = tc * 512

        def _(t, cs=cs, tc=tc):
            if tc == 0:
                t.wait_ge(sB.h, B_P3)
            else:
                t.wait_ge(sV.h, PH_V[tc - 1])
            sT.inc(t.matmul(psBB[:, :], K["onesbrow"][:, :], rs1b[:, cs:cs + 512],
                            start=True, stop=True))
        TEN(_)
        T_RSB = sT.n

        def _(v, cs=cs, T_RSB=T_RSB):
            v.wait_ge(sT.h, T_RSB)
            for fc in range(NFC):
                v.scalar_tensor_tensor(out=ph[fc][:, :], in0=xn_s(fc, cs, cs + 512),
                                       scalar=VCOL["pnw"][:, fc:fc + 1],
                                       in1=psBB[:, :], op0=Alu.mult, op1=Alu.mult)
        VEC(_)
        PH_V = PH_V if tc else [None] * NTC
        PH_V[tc] = sV.n

        MU_S = [None] * NFC
        for doc in range(NFC):
            def _(t, doc=doc, tc=tc):
                if doc == 0:
                    t.wait_ge(sW.h, W_PR)
                    t.wait_ge(sV.h, PH_V[tc])
                if doc >= 2:
                    t.wait_ge(sS.h, MU_S[doc - 2])
                for dinc in range(NFC):
                    ins = t.matmul(psP[:, (doc % 2) * 512:(doc % 2) * 512 + 512],
                                   PR[dinc][:, doc * 128:(doc + 1) * 128],
                                   ph[dinc][:, :],
                                   start=(dinc == 0), stop=(dinc == NFC - 1))
                sT.inc(ins)
            TEN(_)
            mmu = sT.n

            def _(s, doc=doc, mmu=mmu):
                s.wait_ge(sT.h, mmu)
                s.copy(out=mu_sb[doc][:, :],
                       in_=psP[:, (doc % 2) * 512:(doc % 2) * 512 + 512])
            SCA(_)
            MU_S[doc] = sS.n

        LV_DONE = [None] * NFC
        CW_T = [None] * NFC
        if tc == 0:
            CW_EV = [None] * NTC
        for fc in range(NFC):
            doc = 8 + fc

            def _(t, fc=fc, doc=doc):
                if fc < 2:
                    t.wait_ge(sS.h, MU_S[6 + fc])
                else:
                    t.wait_ge(sV.h, LV_DONE[fc - 2])
                for dinc in range(NFC):
                    ins = t.matmul(psP[:, (fc % 2) * 512:(fc % 2) * 512 + 512],
                                   PR[dinc][:, doc * 128:(doc + 1) * 128],
                                   ph[dinc][:, :],
                                   start=(dinc == 0), stop=(dinc == NFC - 1))
                sT.inc(ins)
            TEN(_)
            mlv = sT.n

            def _(v, fc=fc, cs=cs, mlv=mlv):
                v.wait_ge(sT.h, mlv)
                v.tensor_tensor(out=ascr[:, :], in0=sum_s(fc, cs, cs + 512),
                                in1=mu_sb[fc][:, :], op=Alu.subtract)
            VEC(_)
            V_A = sV.n

            def _(s, fc=fc, mlv=mlv, V_A=V_A):
                s.wait_ge(sT.h, mlv)
                s.activation(out=escr[:, :],
                             in_=psP[:, (fc % 2) * 512:(fc % 2) * 512 + 512],
                             func=AFT.Exp, scale=-1.0)
                s.wait_ge(sV.h, V_A)
                s.activation(out=ascr[:, :], in_=ascr[:, :], func=AFT.Square)
            SCA(_)
            S_AE = sS.n

            def _(v, fc=fc, cs=cs, S_AE=S_AE):
                v.wait_ge(sS.h, S_AE)
                if fc >= 2:
                    v.wait_ge(sT.h, CW_T[fc - 2])
                v.tensor_tensor(out=t1scr[:, :], in0=ascr[:, :], in1=escr[:, :],
                                op=Alu.mult)
                v.tensor_tensor(out=t1scr[:, :], in0=t1scr[:, :], in1=escr[:, :],
                                op=Alu.add)
                v.tensor_tensor(out=t1scr[:, :], in0=t1scr[:, :],
                                in1=psP[:, (fc % 2) * 512:(fc % 2) * 512 + 512],
                                op=Alu.add)
                v.tensor_scalar(out=ubf[fc % 2][:, :], in0=t1scr[:, :], scalar1=0.5,
                                scalar2=-0.5, op0=Alu.mult, op1=Alu.add)
            VEC(_)
            V_U = sV.n
            LV_DONE[fc] = sV.n

            def _(s, fc=fc, cs=cs, V_U=V_U):
                if fc >= 2:
                    s.wait_ge(sT.h, CW_T[fc - 2])
                s.activation(out=ubf2[fc % 2][:, :], in_=sum_s(fc, cs, cs + 512),
                             func=AFT.Square)
            SCA(_)
            S_U2 = sS.n

            def _(t, fc=fc, V_U=V_U, S_U2=S_U2, tc=tc):
                t.wait_ge(sV.h, V_U)
                t.wait_ge(sS.h, S_U2)
                if fc == 0 and tc >= 1:
                    t.wait_ge(sV.h, CW_EV[tc - 1])
                t.matmul(psCw[:, :], K["onesb"][:, 0:1], ubf[fc % 2][:, :],
                         start=(fc == 0), stop=False)
                ins = t.matmul(psCw[:, :], K["onesb"][:, 0:1], ubf2[fc % 2][:, :],
                               start=False, stop=(fc == NFC - 1))
                sT.inc(ins)
            TEN(_)
            CW_T[fc] = sT.n
        T_CW = sT.n

        def _(v, cs=cs, T_CW=T_CW):
            v.wait_ge(sT.h, T_CW)
            v.tensor_scalar(out=crow[:, cs:cs + 512], in0=psCw[:, :],
                            scalar1=1.0 / D, scalar2=None, op0=Alu.mult,
                            op1=Alu.bypass)
        VEC(_)
        CW_EV[tc] = sV.n
    V_CROW = sV.n

    def _(t):
        t.wait_ge(sV.h, V_CROW)
        for i in range(NTT):
            ins = t.transpose(psCC[:, i:i + 1], crow[0:1, i * 128:(i + 1) * 128],
                              K["ident_f32"][0:1, 0:1])
        sT.inc(ins)
    TEN(_)
    T_CCOL = sT.n

    def _(v):
        v.wait_ge(sT.h, T_CCOL)
        v.tensor_copy(out=ccol[:, :], in_=psCC[:, :])
        v.tensor_reduce(out=ccsb[0:1, 0:1], in_=crow[:, :], axis=AX.X, op=Alu.add)
    VEC(_)
    V_CCOL = sV.n

    sCC = Ctr(es, nc, "sCC")

    def _(g):
        g.wait_ge(sV.h, V_CCOL)
        g.memset(ccsb[0:1, 1:128], 0.0)
        g.wait_ge(sG.h, sG.n)
        sCC.inc(g.dma_start(out=cc_in.ap(), in_=ccsb[:, :]), 16)
        g.wait_ge(sCC.h, sCC.n)
        g.collective_compute(
            "AllReduce", Alu.add, replica_groups=[list(range(ncores))],
            ins=[cc_in.ap()], outs=[cc_out.ap()],
        ).then_inc(sC.h, 1)
        sC.n += 1
        g.wait_ge(sC.h, sC.n)
        sCC.inc(g.dma_start(out=ccsb[:, :], in_=cc_out.ap()), 16)
    GPS(_)
    W_CC = sCC.n
    G_CC = sG.n

    kb.mkdbg("crow", [1, 1, T])
    kb.dbg_dump("crow", [crow], gate=lambda g_: g_.wait_ge(sV.h, V_CCOL))
    p3.close()
    p5t = ExitStack()
    qnrow = p5t.enter_context(nc.sbuf_tensor("qnrow", [1, 1024], F32, side="right"))
    qtk = [p5t.enter_context(nc.sbuf_tensor(f"qtk{i}", [128, 1024], BF16, side="right"))
           for i in range(2)]
    qgk = [p5t.enter_context(nc.sbuf_tensor(f"qgk{i}", [128, 1024], BF16, side="right"))
           for i in range(2)]

    p46 = ExitStack()
    PT = [p46.enter_context(nc.sbuf_tensor(f"PT{i}", [128, 512], BF16)) for i in range(NTT)]
    p4a = ExitStack()
    KS = {}
    for nm in ("stairbig", "riota", "thrc", "triu128", "ones128"):
        arr = cns[nm]
        KS[nm] = p4a.enter_context(nc.sbuf_tensor("ks_" + nm, list(arr.shape),
                                                  BF16 if arr.dtype == BF else F32))
    scs = p4a.enter_context(nc.sbuf_tensor("scs", [128, T], BF16))
    sqs = p4a.enter_context(nc.sbuf_tensor("sqs", [128, T], BF16))
    accd = p4a.enter_context(nc.sbuf_tensor("accd", [128, T], BF16))
    st4 = p4a.enter_context(nc.sbuf_tensor("st4", [128, 8 * NTT], F32))
    psG4 = p4a.enter_context(nc.psum_tensor("psG4", [128, 2048], F32))
    psC4 = p4a.enter_context(nc.psum_tensor("psC4", [128, NTT], F32))
    CNT1 = lambda i: st4[:, i:i + 1]
    C2 = lambda i: st4[:, 16 + i:17 + i]
    MC = lambda i: st4[:, 48 + i:49 + i]
    NMC = lambda i: st4[:, 64 + i:65 + i]
    CUM = lambda i: st4[:, 80 + i:81 + i]
    NCUM = lambda i: st4[:, 96 + i:97 + i]
    NEGC = lambda i: st4[:, 112 + i:113 + i]

    _tp3, _vp3 = sT.n, sV.n

    def _(sy, _tp3=_tp3, _vp3=_vp3):
        sy.wait_ge(sT.h, _tp3)
        sy.wait_ge(sV.h, _vp3)
        for nm in KS:
            kb.sK2.inc(sy.dma_start(out=KS[nm][:, :], in_=CN[nm]), 16)
    SYN(_)
    K_P4 = kb.sK2.n

    def _(g, _b5=_b1):
        g.wait_ge(sB.h, _b5)
        src_q = W["qT_w"].rearrange("(n p) d -> n p d", p=128)
        for i in range(NFC):
            sW.inc(g.dma_start(out=QW[i][:, :], in_=src_q[i]), 16)
        sW.inc(g.dma_start(out=qnrow[:, :], in_=W["qnw"]), 16)
    GPS(_)
    W_Q5 = sW.n

    WB = {}
    PC = {}
    for nm in ("wv2", "wq2", "wk2", "wo2", "wg2", "wu2", "wd2"):
        WB[nm] = nc.dram_tensor("bfw_" + nm, list(W[nm].tensor.shape), BF16,
                                kind="Internal").ap()

    def _(g):
        for nm in WB:
            kb.sPC[nm].inc(g.dma_start(out=WB[nm], in_=W[nm]), 16)
            PC[nm] = kb.sPC[nm].n
    GPS(_)

    def _(t):
        t.wait_ge(sW.h, W_Q5)
        for ch in range(2):
            ins = t.matmul(psQ5[:, ch * 512:(ch + 1) * 512], K["ones_f32row"][:, :],
                           qnrow[:, ch * 512:(ch + 1) * 512], start=True, stop=True)
        sT.inc(ins)
    TEN(_)
    T_QNF = sT.n

    def _(v):
        v.wait_ge(sT.h, T_QNF)
        v.tensor_copy(out=qnwf[:, :], in_=psQ5[:, :])
    VEC(_)
    V_QNF = sV.n
    QS_V = {}
    SQ5_L = {}

    def emit_p5_pass1_tile(tt):
        def _(t, tt=tt):
            if tt == 0:
                t.wait_ge(sV.h, V_QNF)
            if tt >= 1:
                t.wait_ge(sV.h, QS_V[tt - 1])
            for dinc in range(NFC):
                for ch in range(2):
                    ins = t.matmul(psQ5[:, ch * 512:(ch + 1) * 512],
                                   xnTall[:, dinc * T + tt * 128:dinc * T + (tt + 1) * 128],
                                   QW[dinc][:, ch * 512:(ch + 1) * 512],
                                   start=(dinc == 0), stop=(dinc == NFC - 1))
            sT.inc(ins)
        TEN(_)
        mq5 = sT.n

        def _(v, tt=tt, mq5=mq5):
            v.wait_ge(sT.h, mq5)
            if tt >= 2:
                v.wait_ge(sS.h, SQ5_L[tt - 2])
            v.tensor_tensor(out=qgk[tt % 2][:, :], in0=psQ5[:, :], in1=qnwf[:, :],
                            op=Alu.mult)
        VEC(_)
        V_QG = sV.n

        def _(s, tt=tt, V_QG=V_QG, mq5=mq5):
            s.wait_ge(sV.h, V_QG)
            s.activation(out=qtk[tt % 2][:, :], in_=psQ5[:, :], func=AFT.Square)
            s.activation(out=qgk[tt % 2][:, :], in_=qgk[tt % 2][:, :], func=AFT.Square)
        SCA(_)
        S_QSQ = sS.n
        SQ5_L[tt] = sS.n

        def _(v, tt=tt, S_QSQ=S_QSQ):
            v.wait_ge(sS.h, S_QSQ)
            v.tensor_reduce(out=st5[:, tt:tt + 1], in_=qtk[tt % 2][:, :],
                            axis=AX.X, op=Alu.add)
            v.tensor_reduce(out=bass.AP(st5, 16 + tt, [[12 * NTT, 128], [16, 4], [1, 1]]),
                            in_=bass.AP(qgk[tt % 2], 0, [[1024, 128], [256, 4], [1, 256]]),
                            axis=AX.X, op=Alu.add)
        VEC(_)
        QS_V[tt] = sV.n

    def _(t):
        t.wait_ge(sV.h, V_CCOL)
        t.wait_ge(sB.h, sB.n)
        for ch in range(NTC):
            ins = t.matmul(psG4[:, ch * 512:(ch + 1) * 512], K["ones_f32row"][:, :],
                           crow[:, ch * 512:(ch + 1) * 512], start=True, stop=True)
        sT.inc(ins)
    TEN(_)
    T_G4 = sT.n

    def _(v):
        v.wait_ge(sV.h, V_CCOL)
        v.tensor_scalar(out=st4[:, 112:128], in0=ccol[:, :], scalar1=-1.0,
                        scalar2=None, op0=Alu.mult, op1=Alu.bypass)
    VEC(_)
    V_NEGC = sV.n

    V_ACC = {}
    for i in range(NTT):
        def _(s, i=i):
            if i == 0:
                s.wait_ge(sT.h, T_G4)
                s.wait_ge(sV.h, V_NEGC)
                s.wait_ge(kb.sK2.h, K_P4)
            else:
                s.wait_ge(sV.h, V_ACC[i - 1])
            s.activation(out=scs[:, :], in_=psG4[:, :], func=AFT.Sign,
                         bias=NEGC(i))
            s.activation(out=sqs[:, :], in_=scs[:, :], func=AFT.Square)
        SCA(_)
        S_SC = sS.n

        def _(v, i=i, S_SC=S_SC):
            v.wait_ge(sS.h, S_SC)
            v.tensor_scalar(out=accd[:, :], in0=scs[:, :], scalar1=0.0,
                            scalar2=None, op0=Alu.max, op1=Alu.add,
                            accum_out=CNT1(i))
            v.scalar_tensor_tensor(out=accd[:, :], in0=sqs[:, :], scalar=1.0,
                                   in1=KS["stairbig"][:, 1920 - 128 * i:3968 - 128 * i],
                                   op0=Alu.mult, op1=Alu.mult, accum_out=C2(i))
        VEC(_)
        V_ACC[i] = sV.n
        emit_p5_pass1_tile(i)
    V_RANK = sV.n

    def _(v):
        v.tensor_tensor(out=st4[:, 32:48], in0=st4[:, 0:16], in1=st4[:, 16:32],
                        op=Alu.subtract)
        v.tensor_tensor(out=st4[:, 32:48], in0=st4[:, 32:48], in1=KS["thrc"][:, :],
                        op=Alu.add)
        v.tensor_scalar(out=st4[:, 32:48], in0=st4[:, 32:48], scalar1=-1.0,
                        scalar2=511.5, op0=Alu.mult, op1=Alu.add)
    VEC(_)
    V_CNT = sV.n

    def _(s):
        s.wait_ge(sV.h, V_CNT)
        s.activation(out=st4[:, 32:48], in_=st4[:, 32:48], func=AFT.Sign)
    SCA(_)
    S_SGN = sS.n

    def _(v):
        v.wait_ge(sS.h, S_SGN)
        v.tensor_scalar(out=st4[:, 48:64], in0=st4[:, 32:48], scalar1=0.5,
                        scalar2=0.5, op0=Alu.mult, op1=Alu.add)
        v.tensor_scalar(out=st4[:, 64:80], in0=st4[:, 48:64], scalar1=-1.0,
                        scalar2=None, op0=Alu.mult, op1=Alu.bypass)
        v.tensor_copy(out=scs[:, 0:16], in_=st4[:, 48:64])
    VEC(_)
    V_M = sV.n

    def _(t):
        t.wait_ge(sV.h, V_M)
        for i in range(NTT):
            for jt in range(i + 1):
                ins = t.matmul(psC4[:, i:i + 1],
                               KS["ones128"][:, :] if jt < i else KS["triu128"][:, :],
                               scs[:, jt:jt + 1],
                               start=(jt == 0), stop=(jt == i))
            sT.inc(ins)
    TEN(_)
    T_CUM = sT.n

    def _(v):
        v.wait_ge(sT.h, T_CUM)
        v.tensor_copy(out=st4[:, 80:96], in_=psC4[:, :])
        v.tensor_scalar(out=st4[:, 96:112], in0=st4[:, 80:96], scalar1=-1.0,
                        scalar2=None, op0=Alu.mult, op1=Alu.bypass)
    VEC(_)
    V_CUM = sV.n

    V_PTB = {}
    for i in range(NTT):
        def _(s, i=i):
            if i == 0:
                s.wait_ge(sV.h, V_CUM)
            if i >= 2:
                s.wait_ge(sV.h, V_PTB[i - 2])
            s.activation(out=scs[:, 512 * (i % 2):512 * (i % 2) + 512],
                         in_=KS["riota"][:, :], func=AFT.Sign, bias=NCUM(i))
            s.activation(out=sqs[:, 512 * (i % 2):512 * (i % 2) + 512],
                         in_=scs[:, 512 * (i % 2):512 * (i % 2) + 512],
                         func=AFT.Square)
        SCA(_)
        S_PT = sS.n

        def _(v, i=i, S_PT=S_PT):
            v.wait_ge(sS.h, S_PT)
            v.tensor_scalar(out=PT[i][:, :],
                            in0=sqs[:, 512 * (i % 2):512 * (i % 2) + 512],
                            scalar1=NMC(i), scalar2=MC(i),
                            op0=Alu.mult, op1=Alu.add)
        VEC(_)
        V_PTB[i] = sV.n
    V_PT = sV.n

    kb.mkdbg("PTdbg", [NTT, 128, 512], BF16)
    kb.dbg_dump("PTdbg", PT, gate=lambda g_: g_.wait_ge(sV.h, V_PT))

    kb.barrier("TSV")
    p5t.close()
    p4a.close()
    p46b = ExitStack()
    proc_tok = [p46b.enter_context(nc.sbuf_tensor(f"ptk{i}", [128, 1024], BF16))
                for i in range(NTT)]
    sc_row = p46b.enter_context(nc.sbuf_tensor("sc_row", [1, T], F32))
    sc_cols = p46b.enter_context(nc.sbuf_tensor("sc_cols", [128, NTT], F32))
    scb_cols = p46b.enter_context(nc.sbuf_tensor("scb_cols", [128, NTT], BF16))
    p4b = ExitStack()
    prcb = [p4b.enter_context(nc.sbuf_tensor(f"prcb{i}", [128, 1024], BF16)) for i in range(2)]
    psT6 = p4b.enter_context(nc.psum_tensor("psT6", [128, 1024], BF16))
    psSel = p4b.enter_context(nc.psum_tensor("psSel", [128, 1024], F32))

    V_P6E = {}
    T_PRC6 = {}
    B_P6A = sB.n
    for i in range(NTT):
        def _(v, i=i):
            if i == 0:
                v.wait_ge(sB.h, B_P6A)
            if i >= 2:
                v.wait_ge(sT.h, T_PRC6[i - 2])
            v.tensor_tensor(
                out=prcb[i % 2][:, :],
                in0=bass.AP(SUM, i * 128, [[NFC * T, 128], [T, NFC], [1, 128]]),
                in1=bass.AP(xnTall, i * 128, [[NFC * T, 128], [T, NFC], [1, 128]]),
                op=Alu.add)
        VEC(_)
        V_PRC = sV.n

        def _(t, i=i, V_PRC=V_PRC):
            t.wait_ge(sV.h, V_PRC)
            if i >= 2:
                t.wait_ge(sV.h, V_P6E[i - 2])
            for fc in range(NFC):
                ins = t.transpose(psT6[:, fc * 128:(fc + 1) * 128],
                                  prcb[i % 2][:, fc * 128:(fc + 1) * 128],
                                  K["ident_bf"][:, :])
            sT.inc(ins)
        TEN(_)
        T_PRC6[i] = sT.n

        def _(v, i=i):
            v.wait_ge(sT.h, T_PRC6[i])
            v.tensor_copy(out=proc_tok[i][:, :], in_=psT6[:, :])
        VEC(_)
        V_P6E[i] = sV.n
    V_PTOK = sV.n

    kb.mkdbg("ptok", [NTT, 128, 1024], BF16)
    kb.dbg_dump("ptok", proc_tok, gate=lambda g_: g_.wait_ge(sV.h, V_PTOK))

    SEL_EV = {}
    for dc in range(NFC):
        def _(t, dc=dc):
            if dc == 0:
                t.wait_ge(sV.h, V_PTOK)
                t.wait_ge(sV.h, V_PT)
            if dc >= 2:
                t.wait_ge(sS.h, SEL_EV[dc - 2])
            for tt in range(NTT):
                ins = t.matmul(psSel[:, (dc % 2) * 512:(dc % 2) * 512 + 512],
                               proc_tok[tt][:, dc * 128:(dc + 1) * 128],
                               PT[tt][:, :],
                               start=(tt == 0), stop=(tt == NTT - 1))
            sT.inc(ins)
        TEN(_)
        mse = sT.n

        def _(s, dc=dc, mse=mse):
            s.wait_ge(sT.h, mse)
            s.copy(out=sel_s(dc, 0, 512), in_=psSel[:, (dc % 2) * 512:(dc % 2) * 512 + 512])
        SCA(_)
        SEL_EV[dc] = sS.n
    S_SEL = sS.n

    kb.mkdbg("selT", [NFC, 128, 512], BF16)
    kb.dbg_dump("selT", [selTb[:, f * 512:(f + 1) * 512] for f in range(NFC)],
                gate=lambda g_: g_.wait_ge(sS.h, S_SEL))

    def _(v):
        v.wait_ge(sCC.h, W_CC)
        v.wait_ge(sG.h, G_CC)
        v.tensor_scalar(out=ccsb[0:1, 1:2], in0=ccsb[0:1, 0:1],
                        scalar1=-1.0 / (2048.0 * ncores), scalar2=None,
                        op0=Alu.mult, op1=Alu.bypass)
    VEC(_)
    V_NM = sV.n

    def _(s):
        s.wait_ge(sV.h, V_NM)
        s.activation(out=sc_row[:, :], in_=crow[:, :], func=AFT.Sigmoid,
                     bias=ccsb[0:1, 1:2])
    SCA(_)
    S_SCR = sS.n

    def _(t):
        t.wait_ge(sS.h, S_SCR)
        t.wait_ge(sS.h, S_SEL)
        for i in range(NTT):
            ins = t.transpose(psSel[:, i:i + 1], sc_row[0:1, i * 128:(i + 1) * 128],
                              K["ident_f32"][0:1, 0:1])
        sT.inc(ins)
    TEN(_)
    T_SCC = sT.n

    def _(v):
        v.wait_ge(sT.h, T_SCC)
        v.tensor_copy(out=sc_cols[:, :], in_=psSel[0:128, 0:NTT])
        v.tensor_copy(out=scb_cols[:, :], in_=psSel[0:128, 0:NTT])
    VEC(_)
    V_SCC = sV.n

    def _(t):
        t.wait_ge(sV.h, V_SCC)
        for rt in range(4):
            for tt in range(NTT):
                ins = t.matmul(psSel[0:128, 512 + rt:513 + rt],
                               PT[tt][:, rt * 128:(rt + 1) * 128],
                               scb_cols[:, tt:tt + 1],
                               start=(tt == 0), stop=(tt == NTT - 1))
        sT.inc(ins)
    TEN(_)
    T_GS = sT.n

    def _(v):
        v.wait_ge(sT.h, T_GS)
        v.tensor_copy(out=GSEL[:, :], in_=psSel[0:128, 512:516])
    VEC(_)
    V_GSEL = sV.n
    kb.mkdbg("gsel", [1, 128, 4])
    kb.dbg_dump("gsel", [GSEL], gate=lambda g_: g_.wait_ge(sV.h, V_GSEL))

    kb.barrier("TSV")
    p4b.close()
    p46b.close()
    pcrow.close()
    B_P5R = sB.n
    p5 = ExitStack()
    qhb = [p5.enter_context(nc.sbuf_tensor(f"qhb{i}", [128, 256], BF16)) for i in range(2)]
    qhT = p5.enter_context(nc.sbuf_tensor("qhT", [128, 2 * 1024], BF16))
    W5 = [p5.enter_context(nc.sbuf_tensor(f"w5_{i}", [128, 512], BF16)) for i in range(8)]
    gz5 = p5.enter_context(nc.sbuf_tensor("gz5", [128, 1024], BF16))
    z5 = [p5.enter_context(nc.sbuf_tensor(f"z5_{i}", [128, 1024], BF16)) for i in range(4)]
    psH5 = p5.enter_context(nc.psum_tensor("psH5", [128, 1024], F32))
    psU5 = p5.enter_context(nc.psum_tensor("psU5", [128, 1024], F32))
    psT5 = p5.enter_context(nc.psum_tensor("psT5", [128, 512], BF16))

    def _(v):
        v.tensor_scalar(out=st5[:, 80:96], in0=st5[:, 0:16], scalar1=1.0 / D,
                        scalar2=1e-6, op0=Alu.mult, op1=Alu.add)
        v.reciprocal(out=st5[:, 80:96], in_=st5[:, 80:96])
    VEC(_)
    V_R5A = sV.n

    def _(s):
        s.wait_ge(sV.h, V_R5A)
        s.activation(out=st5[:, 80:96], in_=st5[:, 80:96], func=AFT.Sqrt)
        s.activation(out=st5[:, 16:80], in_=st5[:, 16:80], func=AFT.Sqrt)
    SCA(_)
    S_R5 = sS.n

    def _(v):
        v.wait_ge(sS.h, S_R5)
        for h in range(4):
            v.tensor_tensor(out=st5[:, 96 + h * 16:112 + h * 16],
                            in0=st5[:, 16 + h * 16:32 + h * 16],
                            in1=st5[:, 80:96], op=Alu.mult)
        v.tensor_scalar(out=st5[:, 96:160], in0=st5[:, 96:160], scalar1=1e-5,
                        scalar2=None, op0=Alu.add, op1=Alu.bypass)
        v.reciprocal(out=st5[:, 96:160], in_=st5[:, 96:160])
        for h in range(4):
            v.tensor_tensor(out=st5[:, 96 + h * 16:112 + h * 16],
                            in0=st5[:, 96 + h * 16:112 + h * 16],
                            in1=st5[:, 80:96], op=Alu.mult)
    VEC(_)
    V_SC5 = sV.n

    HD_T = [None] * 4
    for h in range(4):
        def _(g, h=h):
            if h >= 1:
                g.wait_ge(sT.h, HD_T[h - 1])
            else:
                g.wait_ge(sB.h, B_P5R)
            src0 = W["w0T"].rearrange("(a p) d -> a p d", p=128)
            src2 = W["w2T"].rearrange("(a p) d -> a p d", p=128)
            src1 = W["w1T"].rearrange("(a p) d -> a p d", p=128)
            for c in range(2):
                sW.inc(g.dma_start(out=W5[c][:, :], in_=src0[h * 2 + c]), 16)
                sW.inc(g.dma_start(out=W5[2 + c][:, :], in_=src2[h * 2 + c]), 16)
            for c in range(4):
                sW.inc(g.dma_start(out=W5[4 + c][:, 0:256], in_=src1[h * 4 + c]), 16)
        GPS(_)
        W_H5 = sW.n

        for th2 in range(2):
            c0 = th2 * 1024
            QH_V = {}
            VQH_L = {}
            for tl in range(8):
                tt = th2 * 8 + tl

                def _(t, h=h, tt=tt, tl=tl):
                    if tl == 0 and th2 == 0 and h == 0:
                        t.wait_ge(sV.h, V_SC5)
                    if tl >= 1:
                        t.wait_ge(sV.h, VQH_L[tl - 1])
                    elif h > 0 or th2 > 0:
                        t.wait_ge(sV.h, PREV_VQH)
                    for dinc in range(NFC):
                        ins = t.matmul(psQ5[:, 0:256],
                                       xnTall[:, dinc * T + tt * 128:dinc * T + (tt + 1) * 128],
                                       QW[dinc][:, h * 256:(h + 1) * 256],
                                       start=(dinc == 0), stop=(dinc == NFC - 1))
                    sT.inc(ins)
                TEN(_)
                mqh = sT.n

                def _(v, h=h, tt=tt, tl=tl, mqh=mqh):
                    v.wait_ge(sT.h, mqh)
                    v.tensor_tensor(out=qhb[tl % 2][:, :], in0=psQ5[:, 0:256],
                                    in1=qnwf[:, h * 256:(h + 1) * 256], op=Alu.mult)
                    v.tensor_scalar(out=qhb[tl % 2][:, :], in0=qhb[tl % 2][:, :],
                                    scalar1=st5[:, 96 + h * 16 + tt:97 + h * 16 + tt],
                                    scalar2=None, op0=Alu.mult, op1=Alu.bypass)
                VEC(_)
                V_QH = sV.n
                VQH_L[tl] = sV.n

                def _(t, tl=tl, V_QH=V_QH):
                    t.wait_ge(sV.h, V_QH)
                    for j in range(2):
                        ins = t.transpose(psT5[:, j * 128:(j + 1) * 128],
                                          qhb[tl % 2][:, j * 128:(j + 1) * 128],
                                          K["ident_bf"][:, :])
                    sT.inc(ins)
                TEN(_)
                tqh = sT.n

                def _(v, tl=tl, tqh=tqh):
                    v.wait_ge(sT.h, tqh)
                    v.tensor_copy(
                        out=bass.AP(qhT, tl * 128, [[2 * 1024, 128], [1024, 2], [1, 128]]),
                        in_=psT5[:, 0:256])
                VEC(_)
                QH_V[tl] = sV.n
            PREV_VQH = sV.n
            V_QHT = sV.n

            Z_V = {}
            for mt in range(4):
                def _(t, h=h, mt=mt, V_QHT=V_QHT, W_H5=W_H5):
                    t.wait_ge(sV.h, V_QHT)
                    t.wait_ge(sW.h, W_H5)
                    if mt >= 1:
                        t.wait_ge(sV.h, Z_V[mt - 1])
                    elif th2 == 1 or h > 0:
                        t.wait_ge(sV.h, PREV_Z)
                    for kc in range(2):
                        for ch in range(2):
                            t.matmul(psH5[:, ch * 512:(ch + 1) * 512],
                                     W5[kc][:, mt * 128:(mt + 1) * 128],
                                     qhT[:, kc * 1024 + ch * 512:kc * 1024 + (ch + 1) * 512],
                                     start=(kc == 0), stop=(kc == 1))
                            ins = t.matmul(psU5[:, ch * 512:(ch + 1) * 512],
                                           W5[2 + kc][:, mt * 128:(mt + 1) * 128],
                                           qhT[:, kc * 1024 + ch * 512:kc * 1024 + (ch + 1) * 512],
                                           start=(kc == 0), stop=(kc == 1))
                    sT.inc(ins)
                TEN(_)
                mgu = sT.n

                def _(s, mt=mt, mgu=mgu):
                    s.wait_ge(sT.h, mgu)
                    s.activation(out=gz5[:, :], in_=psH5[:, :], func=AFT.Silu)
                SCA(_)
                S_GZ = sS.n

                def _(v, mt=mt, S_GZ=S_GZ):
                    v.wait_ge(sS.h, S_GZ)
                    v.tensor_tensor(out=z5[mt][:, :], in0=psU5[:, :], in1=gz5[:, :],
                                    op=Alu.mult)
                VEC(_)
                Z_V[mt] = sV.n

            AH_V = {}
            for mt2 in range(2):
                def _(t, h=h, mt2=mt2):
                    t.wait_ge(sV.h, Z_V[3])
                    if mt2 == 1:
                        t.wait_ge(sV.h, AH_V[0])
                    for kc in range(4):
                        for ch in range(2):
                            ins = t.matmul(psH5[:, ch * 512:(ch + 1) * 512],
                                           W5[4 + kc][:, mt2 * 128:mt2 * 128 + 128],
                                           z5[kc][:, ch * 512:(ch + 1) * 512],
                                           start=(kc == 0), stop=(kc == 3))
                    sT.inc(ins)
                TEN(_)
                mah = sT.n

                def _(v, h=h, mt2=mt2, mah=mah, c0=c0):
                    v.wait_ge(sT.h, mah)
                    fc = h * 2 + mt2
                    v.tensor_tensor(out=SUM[:, fc * T + c0:fc * T + c0 + 1024],
                                    in0=psH5[:, :], in1=SUM[:, fc * T + c0:fc * T + c0 + 1024],
                                    op=Alu.add)
                VEC(_)
                AH_V[mt2] = sV.n
            PREV_Z = sV.n
        HD_T[h] = sT.n
    V_TTT = sV.n

    kb.mkdbg("sum_ttt", [NFC, 128, T])
    kb.dbg_dump("sum_ttt", [SUM[:, f * T:(f + 1) * T] for f in range(NFC)],
                gate=lambda g_: g_.wait_ge(sV.h, V_TTT))

    kb.barrier("TSV")
    p5.close()
    p5a.close()
    prts = ExitStack()
    cosF2 = prts.enter_context(nc.sbuf_tensor("cosF2", [128, 512], BF16, side="right"))
    sinF2 = prts.enter_context(nc.sbuf_tensor("sinF2", [128, 512], BF16, side="right"))
    P_ = prts.enter_context(nc.sbuf_tensor("P_", [128, 4 * T], BF16, side="right"))
    p6b2 = ExitStack()
    KS2 = {}
    for nm in ("costok", "sintok"):
        KS2[nm] = p6b2.enter_context(nc.sbuf_tensor("k2_" + nm, [128, NTT * 32], BF16))
    psM = p6b2.enter_context(nc.psum_tensor("psM", [128, 1024], F32))
    psM2 = p6b2.enter_context(nc.psum_tensor("psM2", [128, 512], BF16))

    def _(g):
        g.wait_ge(sB.h, sB.n)
        src_c = CN["costok"].rearrange("(n p) j -> n p j", p=128)
        src_s = CN["sintok"].rearrange("(n p) j -> n p j", p=128)
        for i in range(NTT):
            sK.inc(g.dma_start(out=KS2["costok"][:, i * 32:(i + 1) * 32], in_=src_c[i]), 16)
            sK.inc(g.dma_start(out=KS2["sintok"][:, i * 32:(i + 1) * 32], in_=src_s[i]), 16)
    GPS(_)
    K_P6 = sK.n

    def _(t):
        t.wait_ge(sK.h, K_P6)
        t.wait_ge(sB.h, sB.n)
        for tt in range(NTT):
            t.matmul(psM[0:32, 0:512], KS2["costok"][:, tt * 32:(tt + 1) * 32],
                     PT[tt][:, :], start=(tt == 0), stop=(tt == NTT - 1))
        for tt in range(NTT):
            ins = t.matmul(psM[0:32, 512:1024], KS2["sintok"][:, tt * 32:(tt + 1) * 32],
                           PT[tt][:, :], start=(tt == 0), stop=(tt == NTT - 1))
        sT.inc(ins)
    TEN(_)
    T_CG = sT.n

    def _(s):
        s.wait_ge(sT.h, T_CG)
        for r in range(4):
            s.copy(out=cosF2[r * 32:(r + 1) * 32, :], in_=psM[0:32, 0:512])
        for r in range(4):
            s.activation(out=sinF2[r * 32:(r + 1) * 32, :], in_=psM[0:32, 512:1024],
                         func=AFT.Identity, scale=(-1.0 if r % 2 == 0 else 1.0))
    SCA(_)
    S_CG = sS.n

    TP_V = {}
    k6 = 0
    for rt in range(4):
        for gq in range(4):
            def _(t, rt=rt, gq=gq, k6=k6):
                if k6 >= 1:
                    t.wait_ge(sV.h, TP_V[k6 - 1])
                for j in range(4):
                    ins = t.transpose(psM2[:, j * 128:(j + 1) * 128],
                                      PT[gq * 4 + j][:, rt * 128:(rt + 1) * 128],
                                      K["ident_bf"][:, :])
                sT.inc(ins)
            TEN(_)
            T_TP = sT.n

            def _(v, rt=rt, gq=gq, T_TP=T_TP):
                v.wait_ge(sT.h, T_TP)
                v.tensor_copy(out=P_[:, rt * T + gq * 512:rt * T + (gq + 1) * 512],
                              in_=psM2[:, :])
            VEC(_)
            TP_V[k6] = sV.n
            k6 += 1
    V_P = sV.n

    kb.barrier("TSV")
    p6b2.close()
    p46.close()

    T2 = 512
    p6c = ExitStack()
    DL2 = p6c.enter_context(nc.sbuf_tensor("DL2", [128, NFC * T2], BF16))
    dl_s = lambda fc, c0, c1: DL2[:, fc * T2 + c0:fc * T2 + c1]
    p6ca = ExitStack()
    WA2 = [p6ca.enter_context(nc.sbuf_tensor(f"wa2_{i}", [128, 1024], BF16))
           for i in range(NFC)]
    q2T = p6ca.enter_context(nc.sbuf_tensor("q2T", [128, NFC * T2], BF16))
    k2T = p6ca.enter_context(nc.sbuf_tensor("k2T", [128, NFC * T2], BF16))
    v2tok = [p6ca.enter_context(nc.sbuf_tensor(f"v2t{i}", [128, 1040], BF16))
             for i in range(4)]
    qsw2 = p6ca.enter_context(nc.sbuf_tensor("qsw2", [128, 512], BF16))
    ra2 = p6ca.enter_context(nc.sbuf_tensor("ra2", [128, 512], BF16))
    rb2 = p6ca.enter_context(nc.sbuf_tensor("rb2", [128, 512], BF16))
    rs2c6 = p6ca.enter_context(nc.sbuf_tensor("rs2c6", [128, 4], F32))
    rs2row = p6ca.enter_context(nc.sbuf_tensor("rs2row", [1, 512], F32))
    o2sc = [p6ca.enter_context(nc.sbuf_tensor(f"o2sc{f}", [128, 512], BF16))
            for f in range(NFC)]
    expb2 = p6ca.enter_context(nc.sbuf_tensor("expb2", [128, 2048], BF16))
    SR2 = p6ca.enter_context(nc.sbuf_tensor("SR2", [16, 512], F32))
    SR2b = p6ca.enter_context(nc.sbuf_tensor("SR2b", [16, 512], BF16))
    STG2 = None
    psV2 = p6ca.enter_context(nc.psum_tensor("psV2", [128, 1024], F32))
    sq6 = [qsw2, ra2]
    p6cs = ExitStack()
    psR6 = p6cs.enter_context(nc.psum_tensor("psR6", [1, 512], F32))
    psC6 = p6cs.enter_context(nc.psum_tensor("psC6", [128, 4], F32))

    S_SQ6 = sS.n
    T_R6 = sT.n

    def _(s):
        s.wait_ge(sB.h, sB.n)
        for fc in range(NFC):
            if fc >= 2:
                s.wait_ge(sT.h, T_R6 + fc - 1)
            s.activation(out=sq6[fc % 2][:, :], in_=sel_s(fc, 0, T2), func=AFT.Square)
    SCA(_)

    def _(t):
        for fc in range(NFC):
            t.wait_ge(sS.h, S_SQ6 + fc + 1)
            sT.inc(t.matmul(psR6[:, :], K["onesb"][:, 0:1], sq6[fc % 2][:, :],
                            start=(fc == 0), stop=(fc == NFC - 1)))
    TEN(_)
    T_R6E = sT.n

    def _(v):
        v.wait_ge(sT.h, T_R6E)
        v.tensor_scalar(out=rs2row[:, :], in0=psR6[:, :], scalar1=1.0 / D,
                        scalar2=1e-6, op0=Alu.mult, op1=Alu.add)
        v.reciprocal(out=rs2row[:, :], in_=rs2row[:, :])
    VEC(_)
    V_R6A = sV.n

    def _(s):
        s.wait_ge(sV.h, V_R6A)
        s.activation(out=rs2row[:, :], in_=rs2row[:, :], func=AFT.Sqrt)
    SCA(_)
    S_R6 = sS.n

    def _(t):
        t.wait_ge(sS.h, S_R6)
        for i in range(4):
            ins = t.transpose(psC6[:, i:i + 1], rs2row[0:1, i * 128:(i + 1) * 128],
                              K["ident_f32"][0:1, 0:1])
        sT.inc(ins)
        ins = t.matmul(psV2[:, 0:512], K["ones_f32row"][:, :], rs2row[:, :],
                       start=True, stop=True)
        sT.inc(ins)
    TEN(_)
    T_R6C = sT.n

    def _(v):
        v.wait_ge(sT.h, T_R6C)
        v.wait_ge(sS.h, S_CG)
        v.tensor_copy(out=rs2c6[:, :], in_=psC6[:, :])
        v.tensor_tensor(out=cosF2[:, :], in0=cosF2[:, :], in1=psV2[:, 0:512],
                        op=Alu.mult)
        v.tensor_tensor(out=sinF2[:, :], in0=sinF2[:, :], in1=psV2[:, 0:512],
                        op=Alu.mult)
    VEC(_)
    V_SELRS = sV.n

    def _(g):
        g.wait_ge(sB.h, sB.n)
        for i in range(4):
            g.memset(bass.AP(v2tok[i], 64, [[1040, 128], [65, 16]]), 1.0)
    GPS(_)
    G_ONES2 = sG.n
    kb.barrier("TV")
    p6cs.close()
    p6ca2 = ExitStack()
    psA2 = p6ca2.enter_context(nc.psum_tensor("psA2", [128, 2048], F32))
    psO2 = p6ca2.enter_context(nc.psum_tensor("psO2", [65, 1024], F32))

    def emit_dxd_load(wname, gate_t):
        _gb = sB.n

        def _(sy, wname=wname, gate_t=gate_t, _gb=_gb):
            if gate_t is not None:
                sy.wait_ge(sT.h, gate_t)
            else:
                sy.wait_ge(sB.h, _gb)
            sy.wait_ge(kb.sPC[wname].h, PC[wname])
            src_w = WB[wname].rearrange("(n p) d -> n p d", p=128)
            for i in range(NFC):
                kb.sW2.inc(sy.dma_start(out=WA2[i][:, :], in_=src_w[i]), 16)
        SYN(_)
        return kb.sW2.n

    W_V2 = emit_dxd_load("wv2", None)

    def _(v):
        v.wait_ge(kb.sW2.h, W_V2)
        for i in range(NFC):
            v.tensor_scalar(out=WA2[i][:, :], in0=WA2[i][:, :],
                            scalar1=VCOL["ln12"][:, i:i + 1], scalar2=None,
                            op0=Alu.mult, op1=Alu.bypass)
    VEC(_)
    V_WV2 = sV.n

    EVV2 = {}
    for tt4 in range(4):
        def _(t, tt4=tt4):
            if tt4 == 0:
                t.wait_ge(sV.h, V_WV2)
                t.wait_ge(sV.h, V_SELRS)
                t.wait_ge(sG.h, G_ONES2)
            if tt4 >= 1:
                t.wait_ge(sS.h, EVV2[tt4 - 1])
            for dinc in range(NFC):
                for ch in range(2):
                    ins = t.matmul(psV2[:, ch * 512:(ch + 1) * 512],
                                   selTb[:, dinc * T2 + tt4 * 128:dinc * T2 + (tt4 + 1) * 128],
                                   WA2[dinc][:, ch * 512:(ch + 1) * 512],
                                   start=(dinc == 0), stop=(dinc == NFC - 1))
            sT.inc(ins)
        TEN(_)
        mv2 = sT.n

        def _(s, tt4=tt4, mv2=mv2):
            s.wait_ge(sT.h, mv2)
            s.activation(out=ap_v65(v2tok[tt4], 0), in_=psV2[:, 0:512],
                         func=AFT.Identity, scale=rs2c6[:, tt4:tt4 + 1])
            s.activation(out=ap_v65(v2tok[tt4], 1), in_=psV2[:, 512:1024],
                         func=AFT.Identity, scale=rs2c6[:, tt4:tt4 + 1])
        SCA(_)
        EVV2[tt4] = sS.n
    S_V2 = sS.n
    T_V2L = sT.n

    def emit_qk2(wname, outT, gate):
        S0, V0 = sS.n, sV.n
        WL = emit_dxd_load(wname, gate)

        def _(v, WL=WL):
            v.wait_ge(kb.sW2.h, WL)
            for i in range(NFC):
                v.tensor_scalar(out=WA2[i][:, :], in0=WA2[i][:, :],
                                scalar1=VCOL["ln12"][:, i:i + 1], scalar2=None,
                                op0=Alu.mult, op1=Alu.bypass)
        VEC(_)
        V_WS = sV.n

        RP_V = {}
        for dc in range(NFC):
            hf = (dc % 2) * 512

            def _(t, dc=dc, hf=hf, V_WS=V_WS, S0=S0, V0=V0):
                if dc <= 1:
                    t.wait_ge(sV.h, V_WS)
                    t.wait_ge(sS.h, S0)
                    t.wait_ge(sV.h, V0)
                if dc >= 2:
                    t.wait_ge(sV.h, RP_V[dc - 2])
                for dinc in range(NFC):
                    ins = t.matmul(psV2[:, hf:hf + 512],
                                   WA2[dinc][:, dc * 128:(dc + 1) * 128],
                                   selTb[:, dinc * T2:(dinc + 1) * T2],
                                   start=(dinc == 0), stop=(dinc == NFC - 1))
                sT.inc(ins)
            TEN(_)
            mq2 = sT.n

            def _(v, dc=dc, hf=hf, mq2=mq2, outT=outT):
                v.wait_ge(sT.h, mq2)
                v.tensor_copy(out=ra2[:, :], in_=psV2[:, hf:hf + 512])
                for b in range(2):
                    v.tensor_copy(out=qsw2[b * 64:b * 64 + 32, :],
                                  in_=psV2[b * 64 + 32:b * 64 + 64, hf:hf + 512])
                    v.tensor_copy(out=qsw2[b * 64 + 32:b * 64 + 64, :],
                                  in_=psV2[b * 64:b * 64 + 32, hf:hf + 512])
                v.tensor_tensor(out=rb2[:, :], in0=ra2[:, :], in1=cosF2[:, :],
                                op=Alu.mult)
                v.tensor_tensor(out=ra2[:, :], in0=qsw2[:, :], in1=sinF2[:, :],
                                op=Alu.mult)
                v.tensor_tensor(out=outT[:, dc * T2:(dc + 1) * T2], in0=rb2[:, :],
                                in1=ra2[:, :], op=Alu.add)
            VEC(_)
            RP_V[dc] = sV.n
        return sV.n

    V_Q2 = emit_qk2("wq2", q2T, T_V2L)
    TQ2L = sT.n
    V_K2 = emit_qk2("wk2", k2T, TQ2L)
    TK2L = sT.n

    kb.mkdbg("q2T", [NFC, 128, T2], BF16)
    kb.dbg_dump("q2T", [q2T[:, f * T2:(f + 1) * T2] for f in range(NFC)],
                gate=lambda g_: g_.wait_ge(sV.h, V_Q2))
    kb.mkdbg("v2tok", [4, 128, 1040], BF16)
    kb.dbg_dump("v2tok", v2tok, gate=lambda g_: g_.wait_ge(sS.h, S_V2))

    exp_bp2 = {0: 0, 1: 0}
    av_bh = {0: 0, 1: 0}
    evo2 = {}
    evs2 = {}
    stg2_dma = {}
    for h in range(16):
        ng = 2
        ex_d = [None] * ng
        msk_d = [None] * ng
        av_d = [None] * ng
        r0 = (h % 2) * 64
        cb = (h // 2) * T2
        for g_ in range(ng):
            def _(t, h=h, g=g_, r0=r0, cb=cb):
                t.wait_ge(sS.h, exp_bp2[g % 2])
                if h == 0 and g == 0:
                    t.wait_ge(sV.h, V_K2)
                    t.wait_ge(sS.h, S_V2)
                for j, kt in enumerate((2 * g, 2 * g + 1)):
                    ins = t.matmul(
                        psA2[:, ((g % 2) * 2 + j) * 512:((g % 2) * 2 + j + 1) * 512],
                        k2T[r0:r0 + 64, cb + kt * 128:cb + (kt + 1) * 128],
                        q2T[r0:r0 + 64, cb:cb + 512],
                        start=True, stop=True)
                sT.inc(ins)
            TEN(_)
            sc2 = sT.n

            def _(s, g=g_, sc2=sc2, avb=av_bh[g_ % 2]):
                s.wait_ge(sT.h, sc2)
                s.wait_ge(sT.h, avb)
                s.activation(
                    out=expb2[:, (g % 2) * 1024:(g % 2) * 1024 + 1024],
                    in_=psA2[:, (g % 2) * 1024:(g % 2) * 1024 + 1024],
                    func=AFT.Exp, scale=0.125)
            SCA(_)
            ex_d[g_] = sS.n
            exp_bp2[g_ % 2] = sS.n

            def _(v, g=g_):
                v.wait_ge(sS.h, ex_d[g])
                v.tensor_tensor(
                    out=expb2[:, (g % 2) * 1024:(g % 2) * 1024 + 1024],
                    in0=expb2[:, (g % 2) * 1024:(g % 2) * 1024 + 1024],
                    in1=K["diagmask"][:, 2 * g * 512:(2 * g + 2) * 512],
                    op=Alu.mult)
            VEC(_)
            msk_d[g_] = sV.n

            def _(t, h=h, gg=g_, ng=ng):
                t.wait_ge(sV.h, msk_d[gg])
                if gg == 0 and h >= 2:
                    t.wait_ge(sV.h, evo2[h - 2])
                    t.wait_ge(sV.h, evs2[h - 2])
                for j, kt in enumerate((2 * gg, 2 * gg + 1)):
                    ins = t.matmul(
                        psO2[:, (h % 2) * 512:(h % 2) * 512 + 512],
                        v2tok[kt][:, h * 65:h * 65 + 65],
                        expb2[:, (gg % 2) * 1024 + j * 512:(gg % 2) * 1024 + (j + 1) * 512],
                        start=(gg == 0 and j == 0), stop=(gg == ng - 1 and j == 1))
                sT.inc(ins)
            TEN(_)
            av_d[g_] = sT.n
            av_bh[g_ % 2] = sT.n

        AV2 = av_d[ng - 1]

        def _(v, h=h, AV2=AV2):
            v.wait_ge(sT.h, AV2)
            v.tensor_copy(out=o2sc[h // 2][(h % 2) * 64:(h % 2) * 64 + 64, :],
                          in_=psO2[0:64, (h % 2) * 512:(h % 2) * 512 + 512])
        VEC(_)
        evo2[h] = sV.n

        def _(v, h=h, AV2=AV2):
            v.wait_ge(sT.h, AV2)
            if h >= 1:
                v.wait_ge(sX.h, stg2_dma[h - 1])
            v.tensor_copy(out=rs2row[0:1, :],
                          in_=psO2[64:65, (h % 2) * 512:(h % 2) * 512 + 512])
        VEC(_)
        evs2[h] = sV.n

        def _(sy, h=h, ev=evs2[h]):
            sy.wait_ge(sV.h, ev)
            sX.inc(sy.dma_start(out=SR2[h:h + 1, :], in_=rs2row[0:1, :]), 16)
        SYN(_)
        stg2_dma[h] = sX.n
    ATT2_S = sS.n
    SR2_DMA = sX.n

    def _(v):
        v.wait_ge(sX.h, SR2_DMA)
        v.reciprocal(out=SR2[:, :], in_=SR2[:, :])
        v.tensor_copy(out=SR2b[:, :], in_=SR2[:, :])
    VEC(_)
    V_R2 = sV.n

    T_BC2 = sT.n

    def _(t):
        t.wait_ge(sV.h, V_R2)
        for pt in range(NFC):
            if pt >= 2:
                t.wait_ge(sV.h, V_R2 + pt - 1)
            ins = t.matmul(psA2[:, (pt % 2) * 512:(pt % 2) * 512 + 512],
                           K["hs64"][:, pt * 128:(pt + 1) * 128],
                           SR2b[:, :], start=True, stop=True)
            sT.inc(ins)
    TEN(_)

    def _(v):
        for pt in range(NFC):
            v.wait_ge(sT.h, T_BC2 + pt + 1)
            v.tensor_tensor(out=o2sc[pt][:, :], in0=o2sc[pt][:, :],
                            in1=psA2[:, (pt % 2) * 512:(pt % 2) * 512 + 512],
                            op=Alu.mult)
    VEC(_)
    V_O2 = sV.n

    W_WO2 = emit_dxd_load("wo2", TK2L)
    EVD2 = {}
    for fo in range(NFC):
        def _(t, fo=fo):
            if fo == 0:
                t.wait_ge(kb.sW2.h, W_WO2)
                t.wait_ge(sV.h, V_O2)
            if fo >= 2:
                t.wait_ge(sS.h, EVD2[fo - 2])
            for dinc in range(NFC):
                ins = t.matmul(psV2[:, (fo % 2) * 512:(fo % 2) * 512 + 512],
                               WA2[dinc][:, fo * 128:(fo + 1) * 128],
                               o2sc[dinc][:, :],
                               start=(dinc == 0), stop=(dinc == NFC - 1))
            sT.inc(ins)
        TEN(_)
        mwo2 = sT.n

        def _(s, fo=fo, mwo2=mwo2):
            s.wait_ge(sT.h, mwo2)
            s.copy(out=dl_s(fo, 0, T2), in_=psV2[:, (fo % 2) * 512:(fo % 2) * 512 + 512])
        SCA(_)
        EVD2[fo] = sS.n

    kb.barrier("TSV")
    p6ca2.close()
    p6ca.close()

    p6cb = ExitStack()
    WG2 = [p6cb.enter_context(nc.sbuf_tensor(f"wg2_{i}", [128, 1024], BF16)) for i in range(NFC)]
    WU2 = [p6cb.enter_context(nc.sbuf_tensor(f"wu2_{i}", [128, 1024], BF16)) for i in range(NFC)]
    h2T2 = [p6cb.enter_context(nc.sbuf_tensor(f"h2T2_{f}", [128, 512], BF16)) for f in range(NFC)]
    slab2 = [p6cb.enter_context(nc.sbuf_tensor(f"slab2_{f}", [128, 512], BF16)) for f in range(NFC)]
    sqd2 = [p6cb.enter_context(nc.sbuf_tensor(f"sqd2_{i}", [128, 512], BF16)) for i in range(2)]
    gscr2 = p6cb.enter_context(nc.sbuf_tensor("gscr2", [128, 512], BF16))
    h2row2 = p6cb.enter_context(nc.sbuf_tensor("h2row2", [1, 512], F32))
    psG2 = p6cb.enter_context(nc.psum_tensor("psG2", [128, 512], F32))
    psU2 = p6cb.enter_context(nc.psum_tensor("psU2", [128, 512], F32))
    FB2 = dict(WG=WG2, WU=WU2, WD=WG2, x1=slab2, sqd=sqd2, h2T=h2T2, slab=slab2,
               gscr=gscr2, h2row=h2row2, psG=psG2, psU=psU2)
    wg2_r = WB["wg2"].rearrange("(dc p) f -> p dc f", p=128)
    wu2_r = WB["wu2"].rearrange("(dc p) f -> p dc f", p=128)
    wd2_r = WB["wd2"].rearrange("(n p) d -> n p d", p=128)
    emit_ffn(T2, wg2_r, wu2_r, wd2_r, "ln22",
             lambda fc, cs: dl_s(fc, cs, cs + 512),
             lambda fc, cs: sel_s(fc, cs, cs + 512), tag=1, FB=FB2, wd_late=True,
             hw=True)

    kb.barrier("TSV")
    p6cb.close()
    kb.mkdbg("dl2", [NFC, 128, T2], BF16)
    _b6 = sB.n
    kb.dbg_dump("dl2", [DL2[:, f * T2:(f + 1) * T2] for f in range(NFC)],
                gate=lambda g_: g_.wait_ge(sB.h, _b6))

    p6d = ExitStack()
    dtok = [p6d.enter_context(nc.sbuf_tensor(f"dtok{i}", [128, 1024], BF16)) for i in range(4)]
    psD6 = p6d.enter_context(nc.psum_tensor("psD6", [128, 1024], BF16))
    psS6 = p6d.enter_context(nc.psum_tensor("psS6", [128, 1024], F32))

    DT_V = {}
    for rt in range(4):
        def _(t, rt=rt):
            if rt == 0:
                t.wait_ge(sB.h, sB.n)
            if rt >= 1:
                t.wait_ge(sV.h, DT_V[rt - 1])
            for fc in range(NFC):
                ins = t.transpose(psD6[:, fc * 128:(fc + 1) * 128],
                                  DL2[:, fc * T2 + rt * 128:fc * T2 + (rt + 1) * 128],
                                  K["ident_bf"][:, :])
            sT.inc(ins)
        TEN(_)
        tdt = sT.n

        def _(v, rt=rt, tdt=tdt):
            v.wait_ge(sT.h, tdt)
            v.wait_ge(sV.h, V_GSEL)
            v.tensor_scalar(out=dtok[rt][:, :], in0=psD6[:, :],
                            scalar1=GSEL[:, rt:rt + 1], scalar2=None,
                            op0=Alu.mult, op1=Alu.bypass)
        VEC(_)
        DT_V[rt] = sV.n
    V_DT = sV.n

    SC_V = {}
    ksc = 0
    for dc in range(NFC):
        for tcc in range(NTC):
            def _(t, dc=dc, tcc=tcc, ksc=ksc):
                if ksc == 0:
                    t.wait_ge(sV.h, V_DT)
                if ksc >= 2:
                    t.wait_ge(sV.h, SC_V[ksc - 2])
                for rt in range(4):
                    ins = t.matmul(psS6[:, (ksc % 2) * 512:(ksc % 2) * 512 + 512],
                                   dtok[rt][:, dc * 128:(dc + 1) * 128],
                                   P_[:, rt * T + tcc * 512:rt * T + (tcc + 1) * 512],
                                   start=(rt == 0), stop=(rt == 3))
                sT.inc(ins)
            TEN(_)
            tsc = sT.n

            def _(v, dc=dc, tcc=tcc, ksc=ksc, tsc=tsc):
                v.wait_ge(sT.h, tsc)
                v.tensor_tensor(out=sum_s(dc, tcc * 512, (tcc + 1) * 512),
                                in0=psS6[:, (ksc % 2) * 512:(ksc % 2) * 512 + 512],
                                in1=sum_s(dc, tcc * 512, (tcc + 1) * 512), op=Alu.add)
            VEC(_)
            SC_V[ksc] = sV.n
            ksc += 1
    V_SCAT = sV.n

    kb.barrier("TSV")
    p6d.close()
    p6c.close()
    prts.close()
    pselt.close()

    p7 = ExitStack()
    x7 = [p7.enter_context(nc.sbuf_tensor(f"x7_{i}", [128, 1024], F32)) for i in range(2)]
    t7 = [p7.enter_context(nc.sbuf_tensor(f"t7_{i}", [128, 1024], F32))
          for i in range(2)]
    ob7 = [p7.enter_context(nc.sbuf_tensor(f"ob7_{i}", [128, 1024], F32)) for i in range(2)]
    psF = p7.enter_context(nc.psum_tensor("psF", [128, 2048], F32))

    V_F0 = sV.n
    T_F0 = sT.n
    B_F = sB.n

    def _(sy):
        sy.wait_ge(sB.h, B_F)
        sDX[0].inc(sy.dma_start(out=x7[0][:, :], in_=x_t[0]), 16)
        sDX[1].inc(sy.dma_start(out=x7[1][:, :], in_=x_t[1]), 16)
    SYN(_)
    xf = {0: sDX[0].n, 1: sDX[1].n}

    OUT_E = {}
    OUT_V = {}
    OUT_G = {}
    OUT_D = {}
    ot = out_h.rearrange("(n p) d -> n p d", p=128)
    for i in range(NTT):
        def _(t, i=i):
            if i == 0:
                t.wait_ge(sB.h, B_F)
            if i >= 2:
                t.wait_ge(sV.h, OUT_V[i - 2])
            half = (i % 2) * 1024
            for fc in range(NFC):
                ins = t.transpose(psF[:, half + fc * 128:half + (fc + 1) * 128],
                                  bass.AP(SUM, fc * T + i * 128, [[NFC * T, 128], [1, 128]]),
                                  K["ident_f32"][:, :])
            sT.inc(ins)
        TEN(_)
        T_F = sT.n

        def _(v, i=i, T_F=T_F):
            v.wait_ge(sDX[i % 2].h, xf[i])
            v.wait_ge(sT.h, T_F)
            if i >= 2:
                v.wait_ge(sV.h, OUT_G[i - 2])
            half = (i % 2) * 1024
            v.tensor_scalar(out=t7[i % 2][:, :], in0=x7[i % 2][:, :],
                            scalar1=m16[:, i:i + 1], scalar2=r16[:, i:i + 1],
                            op0=Alu.subtract, op1=Alu.mult)
            v.tensor_tensor(out=t7[i % 2][:, :], in0=t7[i % 2][:, :],
                            in1=psF[:, half:half + 1024], op=Alu.add)
            OUT_V[i] = None
            if i >= 2:
                sXP = [sX, kb.sXo][i % 2]
                v.wait_ge(sXP.h, OUT_D[i - 2])
            v.tensor_tensor(out=ob7[i % 2][:, :], in0=t7[i % 2][:, :],
                            in1=x7[i % 2][:, :], op=Alu.add)
        VEC(_)
        OUT_V[i] = sV.n - 1
        OUT_G[i] = sV.n
        OUT_E[i] = sV.n

        if i % 2 == 0:
            def _(sy, i=i):
                sy.wait_ge(sV.h, OUT_E[i])
                sX.inc(sy.dma_start(out=ot[i], in_=ob7[0][:, :]), 16)
                if i + 2 < NTT:
                    sDX[0].inc(sy.dma_start(out=x7[0][:, :], in_=x_t[i + 2]), 16)
            SYN(_)
        else:
            def _(g, i=i):
                g.wait_ge(sV.h, OUT_E[i])
                kb.sXo.inc(g.dma_start(out=ot[i], in_=ob7[1][:, :]), 16)
            GPS(_)

            def _(sy, i=i):
                sy.wait_ge(sV.h, OUT_E[i])
                if i + 2 < NTT:
                    sDX[1].inc(sy.dma_start(out=x7[1][:, :], in_=x_t[i + 2]), 16)
            SYN(_)
        OUT_D[i] = [sX, kb.sXo][i % 2].n
        if i + 2 < NTT:
            xf[i + 2] = sDX[i % 2].n

    def _(sy):
        sy.wait_ge(sX.h, sX.n)
        sy.wait_ge(kb.sXo.h, kb.sXo.n)
    SYN(_)

    p7.close()
    kb.flush()
    es.close()
    return nc, kb.dbg_out, cns
_DBG = ()


def kernel(**inputs):
    g = lambda n: np.ascontiguousarray(np.asarray(inputs[n], dtype=np.float32))
    x_full = g("L0_attn_hidden_states")
    qkv_w = g("L0_attn_qkv_w")
    w0 = g("L0_attn_w0")
    w1 = g("L0_attn_w1")
    w2 = g("L0_attn_w2")
    wmap = {
        "qT_w": np.ascontiguousarray(qkv_w[:D].T),
        "qnw": g("L0_attn_q_norm_w").reshape(1, D),
        "w0T": np.ascontiguousarray(w0.transpose(0, 2, 1)).reshape(4 * 256, 512),
        "w1T": np.ascontiguousarray(w1.transpose(0, 2, 1)).reshape(4 * 512, 256),
        "w2T": np.ascontiguousarray(w2.transpose(0, 2, 1)).reshape(4 * 256, 512),
        "wq1": g("L0_ffn_wq1"), "wk1": g("L0_ffn_wk1"), "wv1": g("L0_ffn_wv1"),
        "wo1": g("L0_ffn_wo1"),
        "ln11": g("L0_ffn_ln1_1").reshape(1, D), "ln21": g("L0_ffn_ln2_1").reshape(1, D),
        "wg1": g("L0_ffn_wg1"), "wu1": g("L0_ffn_wu1"), "wd1": g("L0_ffn_wd1"),
        "wq2": g("L0_ffn_wq2"), "wk2": g("L0_ffn_wk2"), "wv2": g("L0_ffn_wv2"),
        "wo2": g("L0_ffn_wo2"),
        "ln12": g("L0_ffn_ln1_2").reshape(1, D), "ln22": g("L0_ffn_ln2_2").reshape(1, D),
        "wg2": g("L0_ffn_wg2"), "wu2": g("L0_ffn_wu2"), "wd2": g("L0_ffn_wd2"),
        "pnw": g("L0_ffn_prior_norm_w").reshape(1, D),
        "prior_w": g("L0_ffn_prior_w"),
    }
    nc, dbg_out, cns = build_program(_DBG)
    cmap = {"c_" + nm: arr for nm, arr in cns.items()}
    in_maps = []
    for c in range(8):
        m = {"x": np.ascontiguousarray(x_full[c])}
        m.update(wmap)
        m.update(cmap)
        in_maps.append(m)
    res = run_bass_kernel_spmd(nc, in_maps, core_ids=list(range(8)))
    global _DBG_RES
    _DBG_RES = res.results
    out = np.stack([res.results[c]["out"] for c in range(8)], axis=0)
    return out.astype(np.float32)


_DBG_RES = None
